# version 01 (final)
# baseline (speedup 1.0000x reference)
import numpy as np
import ml_dtypes
import concourse.bass as bass
import concourse.mybir as mybir
from concourse.bass_utils import run_bass_kernel_spmd

F32 = mybir.dt.float32
BF16 = mybir.dt.bfloat16
I32 = mybir.dt.int32
U32 = mybir.dt.uint32
AF = mybir.ActivationFunctionType
ALU = mybir.AluOpType
AX = mybir.AxisListType

T = 2048
BISECT_NO_HEADS = False
ATTN_NG = 4
ATTN_REDIRECT_G1 = False
D = 1024
NT = 16
ND = 8


ENGINES = ("sync", "scalar", "vector", "tensor", "gpsimd")


class Prog:
    def __init__(self, nc):
        self.nc = nc
        self.q = {e: [] for e in ENGINES}
        self.counts = {}
        self.sem_handles = {}
        self.hwm = {e: {} for e in ENGINES}

    def sem(self, name):
        if name not in self.counts:
            self.counts[name] = 0
        return name

    def emit(self, eng, fn, waits=(), inc=None, inc_amt=1):
        flat = []
        def _fl(w):
            if w is None or len(w) == 0:
                return
            if isinstance(w[0], str):
                flat.append(w)
            else:
                for e in w:
                    _fl(e)
        for w in waits:
            _fl(w)
        self.q[eng].append((fn, tuple(flat), inc, inc_amt))
        if inc is not None:
            self.sem(inc)
            self.counts[inc] += inc_amt
            return (inc, self.counts[inc])
        return None

    def barrier_token(self, *tokens):
        return tuple(t for t in tokens if t is not None)

    def build_block(self):
        nc = self.nc
        block = nc.Block()
        block.__enter__()
        prog = self

        def mk(eng_name):
            def body(eng):
                hwm = prog.hwm[eng_name]
                for fn, waits, inc, inc_amt in prog.q[eng_name]:
                    for sem_name, val in waits:
                        if hwm.get(sem_name, 0) < val:
                            eng.wait_ge(prog.sem_handles[sem_name], val)
                            hwm[sem_name] = val
                    inst = fn(eng)
                    if inc is not None:
                        if inst is None:
                            inst = eng.nop()
                        inst.then_inc(prog.sem_handles[inc], inc_amt)
            return body

        for e in ENGINES:
            if prog.q[e]:
                getattr(block, e)(mk(e))
        return block


def flat(tok_or_none):
    return tok_or_none



def host_constants():
    c = {}
    ident = np.eye(128, dtype=np.float32)
    c["ident_f32"] = ident
    c["ident_bf16"] = ident.astype(ml_dtypes.bfloat16)
    triu_incl = np.triu(np.ones((128, 128), np.float32))
    c["triu_incl_bf16"] = triu_incl.astype(ml_dtypes.bfloat16)
    c["triu_incl_f32"] = triu_incl
    c["triu_strict_f32"] = np.triu(np.ones((128, 128), np.float32), 1)
    c["ones_128_f32"] = np.ones((128, 128), np.float32)
    half = 32
    inv_freq = (1.0 / (10000.0 ** (np.arange(half, dtype=np.float32) / half)))
    pos = np.arange(T, dtype=np.float32)
    freqs = pos[:, None] * inv_freq[None, :]
    cos = np.cos(freqs).T
    sin = np.sin(freqs).T
    SCALE = float(64 ** (-0.25))
    cos128 = np.tile(cos, (4, 1)) * SCALE
    sin128 = np.concatenate([-sin, sin, -sin, sin], axis=0) * SCALE
    c["rope_cos"] = cos128.astype(ml_dtypes.bfloat16).astype(np.float32)
    c["rope_cos"] = cos128.astype(np.float16)
    c["rope_sin"] = sin128.astype(np.float16)
    c["iota_part"] = np.arange(128, dtype=np.float32).reshape(128, 1)
    F = np.zeros((16, 128), np.float32)
    for k in range(16):
        if k % 2 == 0:
            F[k, :64] = 1.0
        else:
            F[k, 64:] = 1.0
    c["headF"] = F
    E = np.zeros((16, 8), np.float32)
    for k in range(16):
        E[k, k // 2] = 1.0
    c["headE"] = E
    ind = np.zeros((32, 4096), np.float32)
    for r in range(2):
        for hh in range(16):
            ind[r * 16 + hh, r * 2048 + hh * 128:r * 2048 + (hh + 1) * 128] = 1.0
    c["indh"] = ind
    return c


CONSTS = host_constants()



class Ctx:

    def __init__(self, nc, prog, es):
        import contextlib
        self.nc = nc
        self.p = prog
        self.es = es
        self.uid = 0
        self.const_aps = {}

    def name(self, base):
        self.uid += 1
        return f"{base}_{self.uid}"

    def sbuf(self, shape, dtype, base="t"):
        return self.es.enter_context(self.nc.sbuf_tensor(self.name(base), shape, dtype))

    def psum(self, shape, dtype=F32, base="ps"):
        return self.es.enter_context(self.nc.psum_tensor(self.name(base), shape, dtype))

    def psum_scoped(self, shape, dtype=F32, base="ps"):
        cm = self.nc.psum_tensor(self.name(base), shape, dtype)
        tile = cm.__enter__()
        return tile, cm

    def sbuf_scoped(self, shape, dtype, base="t"):
        cm = self.nc.sbuf_tensor(self.name(base), shape, dtype)
        tile = cm.__enter__()
        return tile, cm

    def constv(self, value, parts=128):
        if not hasattr(self, "_constv"):
            self._constv = {}
        key = (float(value), parts)
        if key not in self._constv:
            tile = self.sbuf([parts, 1], F32, "cv")
            tok = self.p.emit("vector", lambda v, t=tile, val=float(value): v.memset(t[:], val),
                              inc=self.p.sem("s_dve"))
            self._constv[key] = (tile, tok)
        return self._constv[key]

    def pool(self, name, shape, dtype, base=None):
        if not hasattr(self, "_pool"):
            self._pool = {}
        key = (name, tuple(shape), dtype)
        if key not in self._pool:
            self._pool[key] = self.sbuf(shape, dtype, base or name)
        return self._pool[key]

    def dram(self, shape, dtype, base="d"):
        return self.nc.dram_tensor(self.name(base), shape, dtype)

    NSEM = 4

    def _dma_emit(self, queue, fn, waits):
        if not hasattr(self, "_dmarot"):
            self._dmarot = {}
            self._dmaprev = {}
        lane = self._dmarot.get(queue, 0)
        self._dmarot[queue] = (lane + 1) % self.NSEM
        semn = f"dma_{queue}{lane}"
        prev = self._dmaprev.get((queue, lane))
        tok = self.p.emit(queue, fn, waits=list(waits) + [prev],
                          inc=self.p.sem(semn), inc_amt=16)
        self._dmaprev[(queue, lane)] = tok
        return tok

    def dma(self, queue, out_ap, in_ap, waits=(), **kw):
        return self._dma_emit(queue, lambda e: e.dma_start(out=out_ap, in_=in_ap, **kw),
                              waits)

    def dma_tr(self, queue, out_ap, in_ap, waits=()):
        return self._dma_emit(queue,
                              lambda e: e.dma_start(out=out_ap, in_=in_ap, transpose=True),
                              waits)


def tokens_max(toks):
    return tuple(t for t in toks if t is not None)


def emit_consts(ctx):
    nc = ctx.nc
    c = {}
    c["ident_bf"] = ctx.sbuf([128, 128], BF16, "ident_bf")
    c["ident_f"] = ctx.sbuf([128, 128], F32, "ident_f")
    c["triu_bf"] = ctx.sbuf([128, 128], BF16, "triu_bf")
    toks = []
    for nm, tile in [("ident_bf16", c["ident_bf"]), ("ident_f32", c["ident_f"]),
                     ("triu_incl_bf16", c["triu_bf"])]:
        ap = ctx.const_aps[nm]
        toks.append(ctx.dma("sync", tile[:], ap[:, :]))
    c["ones_mean"] = ctx.sbuf([128, 1], BF16, "ones_mean")
    c["ones_bf"] = ctx.sbuf([128, 1], BF16, "ones_bf")
    c["ones_f"] = ctx.sbuf([128, 128], F32, "ones_f")
    c["ones_row"] = ctx.sbuf([1, 512], BF16, "ones_row")
    def setup(v):
        v.memset(c["ones_mean"][:], 1.0 / 1024.0)
        v.memset(c["ones_bf"][:], 1.0)
        v.memset(c["ones_row"][:], 1.0)
        return v.memset(c["ones_f"][:], 1.0)
    tok = ctx.p.emit("vector", setup, inc=ctx.p.sem("s_dve"))
    c["ready"] = tokens_max(toks + [tok])
    return c


def emit_load_x(ctx, consts, x_in_ap):
    p = ctx.p
    x = [ctx.sbuf([128, T], F32, "x") for _ in range(ND)]
    xt0, cmx0 = ctx.sbuf_scoped([128, 1024], F32, "xt")
    xt1, cmx1 = ctx.sbuf_scoped([128, 1024], F32, "xt")
    xt = [xt0, xt1]
    ps0, cm0 = ctx.psum_scoped([128, 128], F32, "pst")
    ps1, cm1 = ctx.psum_scoped([128, 128], F32, "pst")
    pst = [ps0, ps1]
    pe_toks, ev_toks, ld_toks = {}, {}, {}
    for i in range(NT):
        b = i % 2
        w = [pe_toks.get((i - 2) * 8 + 7)]
        ld_toks[i] = ctx.dma("sync", xt[b][:], x_in_ap[i * 128:(i + 1) * 128, :], waits=w)
        for j in range(ND):
            k = i * 8 + j
            jb = k % 2
            def tr(t, b=b, j=j, jb=jb):
                return t.transpose(pst[jb][:], xt[b][:, j * 128:(j + 1) * 128],
                                   consts["ident_f"][:])
            pe_toks[k] = p.emit("tensor", tr,
                                waits=[ld_toks[i], ev_toks.get(k - 2),
                                       consts["ready"]],
                                inc=p.sem("s_pe"))
            def ev(s, i=i, j=j, jb=jb):
                return s.copy(out=x[j][:, i * 128:(i + 1) * 128], in_=pst[jb][:])
            ev_toks[k] = p.emit("scalar", ev, waits=[pe_toks[k]], inc=p.sem("s_act"))
    cm1.__exit__(None, None, None); cm0.__exit__(None, None, None)
    cmx1.__exit__(None, None, None); cmx0.__exit__(None, None, None)
    return x, ev_toks[NT * 8 - 1]


def emit_store_x(ctx, consts, x, out_ap, x_ready, in_dtype=F32):
    p = ctx.p
    ident = consts["ident_f"] if in_dtype == F32 else consts["ident_bf"]
    ot0 = ctx.pool("sc_fA", [128, T // 2], F32)[:, 0:1024]
    ot = [ot0, ot0]
    ps0, cm0 = ctx.psum_scoped([128, 128], in_dtype, "pso")
    ps1, cm1 = ctx.psum_scoped([128, 128], in_dtype, "pso")
    pst = [ps0, ps1]
    pe_toks, ev_toks, st_toks = {}, {}, {}
    for i in range(NT):
        b = i % 2
        for j in range(ND):
            k = i * 8 + j
            jb = k % 2
            def tr(t, i=i, j=j, jb=jb):
                return t.transpose(pst[jb][:], x[j][:, i * 128:(i + 1) * 128],
                                   ident[:])
            pe_toks[k] = p.emit("tensor", tr,
                                waits=[x_ready, ev_toks.get(k - 2)],
                                inc=p.sem("s_pe"))
            def ev(s, b=b, j=j, jb=jb):
                return s.copy(out=ot[b][:, j * 128:(j + 1) * 128], in_=pst[jb][:])
            ev_toks[k] = p.emit("scalar", ev,
                                waits=[pe_toks[k], st_toks.get(i - 1)],
                                inc=p.sem("s_act"))
        st_toks[i] = ctx.dma("sync", out_ap[i * 128:(i + 1) * 128, :], ot[b][:],
                             waits=[ev_toks[i * 8 + 7]])
    cm1.__exit__(None, None, None); cm0.__exit__(None, None, None)
    return tokens_max([st_toks[NT - 1], st_toks[NT - 2]])


def emit_ln(ctx, consts, x, x_ready, out_xn, extra_scale=1.0, eps=1e-5):
    p = ctx.p
    xb = [ctx.pool(f"ln_xb{i}", [128, T // 2], BF16) for i in range(2)]
    x2b = [ctx.pool(f"ln_x2b{i}", [128, T // 2], BF16) for i in range(2)]
    sc_big = ctx.pool("sc_big", [128, T], F32)
    tmp = ctx.pool("sc_fA", [128, T // 2], F32)
    import math
    eps_ap, eps_tok = ctx.constv(eps, 1)
    lsc_ap, lsc_tok = ctx.constv(math.log(extra_scale), 1)
    HW = T // 2
    out_toks = []
    prev_half = {}
    tok = None
    for half in range(2):
        hs = slice(half * HW, (half + 1) * HW)
        pS1, cmS1 = ctx.psum_scoped([1, HW], F32, "ln_s1")
        pS2, cmS2 = ctx.psum_scoped([1, HW], F32, "ln_s2")
        cast_toks = {}
        mm_toks = []
        for j in range(ND):
            jb = j % 2
            def cst(s, j=j, jb=jb, hs=hs):
                s.activation(out=xb[jb][:], in_=x[j][:, hs], func=AF.Copy)
                return s.activation(out=x2b[jb][:], in_=x[j][:, hs], func=AF.Square)
            w = [x_ready, prev_half.get(("cast", jb))] + ([mm_toks[j - 2]] if j >= 2 else [])
            cast_toks[j] = p.emit("scalar", cst, waits=w, inc=p.sem("s_act"))
            def mm(t, j=j, jb=jb):
                last = None
                for c in range(2):
                    sl = slice(c * 512, (c + 1) * 512)
                    t.matmul(pS1[:, sl], consts["ones_mean"][:], xb[jb][:, sl],
                             start=(j == 0), stop=(j == ND - 1))
                    last = t.matmul(pS2[:, sl], consts["ones_mean"][:], x2b[jb][:, sl],
                                    start=(j == 0), stop=(j == ND - 1))
                return last
            mm_toks.append(p.emit("tensor", mm,
                                  waits=[cast_toks[j], prev_half.get("stats_psum")],
                                  inc=p.sem("s_pe")))
        row_m = sc_big[0:1, 0:HW]
        row_rstd = sc_big[0:1, HW:2 * HW]
        t_a = p.emit("vector", lambda v: v.tensor_copy(out=row_m, in_=pS1[:]),
                     waits=[mm_toks[-1], prev_half.get("rows_done")], inc=p.sem("s_dve"))
        t_b = p.emit("vector", lambda v: v.tensor_tensor(out=row_rstd, in0=row_m,
                                                         in1=row_m, op=ALU.mult),
                     waits=[t_a], inc=p.sem("s_dve"))
        t_rv = p.emit("vector", lambda v: v.tensor_tensor(out=row_rstd, in0=pS2[:],
                                                          in1=row_rstd, op=ALU.subtract),
                      waits=[t_b], inc=p.sem("s_dve"))
        t_s1 = p.emit("scalar", lambda s: s.activation(out=row_rstd, in_=row_rstd,
                                                       func=AF.Ln, bias=eps_ap[:]),
                      waits=[t_rv, eps_tok, lsc_tok], inc=p.sem("s_act"))
        t_rs = p.emit("scalar", lambda s: s.activation(out=row_rstd, in_=row_rstd,
                                                       func=AF.Exp, scale=-0.5, bias=lsc_ap[:]),
                      waits=[t_s1], inc=p.sem("s_act"))
        t_mr = p.emit("vector", lambda v: v.tensor_tensor(out=row_m, in0=row_m,
                                                          in1=row_rstd, op=ALU.mult),
                      waits=[t_rs], inc=p.sem("s_dve"))
        cmS2.__exit__(None, None, None); cmS1.__exit__(None, None, None)
        pB1, cmB1 = ctx.psum_scoped([128, HW], F32, "ln_b1")
        pB2, cmB2 = ctx.psum_scoped([128, HW], F32, "ln_b2")
        def bc1(t):
            last = None
            for c in range(2):
                sl = slice(c * 512, (c + 1) * 512)
                last = t.matmul(pB1[:, sl], consts["ones_f"][0:1, :], row_rstd[:, sl],
                                start=True, stop=True)
            return last
        t_b1 = p.emit("tensor", bc1, waits=[t_rs, t_rv], inc=p.sem("s_pe"))
        def bc2(t):
            last = None
            for c in range(2):
                sl = slice(c * 512, (c + 1) * 512)
                last = t.matmul(pB2[:, sl], consts["ones_f"][0:1, :], row_m[:, sl],
                                start=True, stop=True)
            return last
        t_b2 = p.emit("tensor", bc2, waits=[t_mr, t_b1], inc=p.sem("s_pe"))
        for j in range(ND):
            def ap1(v, j=j, hs=hs):
                return v.tensor_tensor(out=tmp[:], in0=x[j][:, hs], in1=pB1[:],
                                       op=ALU.mult)
            t_t1 = p.emit("vector", ap1,
                          waits=[t_b1, t_b2, prev_half.get("tmp_done"),
                                 out_toks[-1] if out_toks else None],
                          inc=p.sem("s_dve"))
            def ap2(v, j=j, hs=hs):
                return v.tensor_tensor(out=out_xn[j][:, hs], in0=tmp[:], in1=pB2[:],
                                       op=ALU.subtract)
            out_toks.append(p.emit("vector", ap2, waits=[t_t1],
                                   inc=p.sem("s_dve")))
        tok = out_toks[-1]
        prev_half["rows_done"] = tok
        prev_half["tmp_done"] = tok
        prev_half["stats_psum"] = tok
        prev_half[("cast", 0)] = mm_toks[-1]
        prev_half[("cast", 1)] = mm_toks[-1]
        cmB2.__exit__(None, None, None); cmB1.__exit__(None, None, None)
    return tok



_MYBIR_DT = {"float32": F32, "bfloat16": BF16, "int32": I32, "uint32": U32,
             "float16": mybir.dt.float16}


def np_to_dt(a):
    return _MYBIR_DT[str(a.dtype)]


def build_nc(body, input_specs, output_specs, const_names=()):
    import contextlib, os
    nc = bass.Bass("TRN2", target_bir_lowering=False,
                   detect_race_conditions=os.environ.get("K_RACED", "1") == "1")
    prog = Prog(nc)
    ins, outs = {}, {}
    for n, (shape, dt) in input_specs.items():
        ins[n] = nc.dram_tensor(n, list(shape), dt, kind="ExternalInput").ap()
    for n, (shape, dt) in output_specs.items():
        outs[n] = nc.dram_tensor(n, list(shape), dt, kind="ExternalOutput").ap()
    const_map = {}
    with contextlib.ExitStack() as es:
        ctx = Ctx(nc, prog, es)
        for cn in dict.fromkeys(const_names):
            arr = CONSTS[cn]
            nm = "c_" + cn
            ctx.const_aps[cn] = nc.dram_tensor(
                nm, list(arr.shape), np_to_dt(np.asarray(arr)), kind="ExternalInput").ap()
            const_map[nm] = np.asarray(arr)
        final = body(ctx, ins, outs)
        for sname in prog.counts:
            prog.sem_handles[sname] = es.enter_context(nc.semaphore(sname))
        prog.emit("gpsimd", lambda e: None, waits=list(final))
        block = prog.build_block()
        block.__exit__(None, None, None)
    return nc, const_map


ALL_CONST_NAMES = ("ident_bf16", "ident_f32", "triu_incl_bf16", "headF", "headE", "rope_cos", "rope_sin")



def emit_prep_bf16(ctx, src_ap, rows, cols, qkv_deinterleave=False):
    dst = ctx.dram([rows, cols], BF16, "wbf")
    if not qkv_deinterleave:
        tok = ctx.dma("gpsimd", dst.ap(), src_ap[:, :])
        return dst, tok
    toks = []
    nh = rows // 192
    for qp in range(3):
        s_view = src_ap.rearrange("(h q d) c -> h q d c", q=3, d=64)[:, qp]
        d_view = dst.ap()[qp * nh * 64:(qp + 1) * nh * 64, :].rearrange(
            "(h d) c -> h d c", d=64)
        toks.append(ctx.dma("gpsimd", d_view, s_view))
    return dst, tokens_max(toks)


def emit_prep_transposed(ctx, src_ap, rows, cols, scale=None, name="wT",
                         qkv_deinterleave=False, bounce=None):
    p = ctx.p
    assert rows % 128 == 0 and cols % 128 == 0
    wbf, cast_tok = emit_prep_bf16(ctx, src_ap, rows, cols,
                                   qkv_deinterleave=qkv_deinterleave)
    wT = ctx.dram([cols, rows], BF16, name)
    nchunk = cols // 128
    wbuf = ctx.pool(bounce, [128, ND * 1024], BF16) if bounce else \
        ctx.pool("ffn_wbuf", [128, ND * 1024], BF16)
    if rows <= 4096:
        tmp = [wbuf[:, 0:rows], wbuf[:, 4096:4096 + rows]]
    else:
        tmp = [wbuf[:, 0:rows], wbuf[:, 0:rows]]
    st_toks, sc_toks = {}, {}
    guard = getattr(ctx, "_prep_guard", [])
    for c in range(nchunk):
        b = c % 2
        w = [cast_tok, st_toks.get(c - 2 if rows <= 4096 else c - 1)] + (guard if c < 2 else [])
        ld = ctx.dma_tr("sync", tmp[b], wbf.ap()[:, c * 128:(c + 1) * 128], waits=w)
        dep = ld
        if scale is not None:
            def sc(v, b=b, s=float(scale)):
                return v.tensor_scalar(out=tmp[b], in0=tmp[b], scalar1=s,
                                       scalar2=None, op0=ALU.mult)
            dep = p.emit("vector", sc, waits=[ld], inc=p.sem("s_dve"))
        st_toks[c] = ctx.dma("scalar", wT.ap()[c * 128:(c + 1) * 128, :], tmp[b],
                             waits=[dep])
    ctx._prep_guard = [st_toks.get(nchunk - 1), st_toks.get(nchunk - 2)]
    return wT, tokens_max([st_toks.get(nchunk - 1), st_toks.get(nchunk - 2)])



def emit_glu_ffn(ctx, consts, x, xn, xn_tok, wgT, wuT, wdT, H, w_tok, tc_width=512,
                 slab_hc=8):
    p = ctx.p
    NH = H // 128
    ntc = T // tc_width
    SLAB = slab_hc * 128
    nslab = (NH + slab_hc - 1) // slab_hc
    wbuf = ctx.pool("ffn_wbuf", [128, ND * 1024], BF16)
    sg = ctx.pool("ffn_sg", [128, tc_width * NH], BF16)
    hb = sg
    wbuf_last_use = {}
    done_toks = []
    for tc in range(ntc):
        tsl = slice(tc * tc_width, (tc + 1) * tc_width)
        pgA, cmA = ctx.psum_scoped([128, tc_width], F32, "ffn_pg")
        pgB, cmB = ctx.psum_scoped([128, tc_width], F32, "ffn_pg")
        pg = [pgA, pgB]
        h_toks = {}
        g_toks = {}
        prev_done = done_toks[-1] if done_toks else None
        ev_hist = [prev_done, prev_done]
        for phase, wsrc in (("g", wgT), ("u", wuT)):
            for sl in range(nslab):
                hc0 = sl * slab_hc
                hcs = list(range(hc0, min(hc0 + slab_hc, NH)))
                ld_toks = []
                for i in range(ND):
                    wv = [w_tok, wbuf_last_use.get(i)]
                    ld_toks.append(ctx.dma(
                        "sync", wbuf[:, i * SLAB:i * SLAB + len(hcs) * 128],
                        wsrc.ap()[i * 128:(i + 1) * 128, hc0 * 128:hc0 * 128 + len(hcs) * 128],
                        waits=wv))
                for hx, hc in enumerate(hcs):
                    bb = len(ev_hist) % 2
                    pg_war = ev_hist[-2] if len(ev_hist) >= 2 else None
                    def mm(t, hx=hx, bb=bb, tsl=tsl, pg=pg):
                        last = None
                        for i in range(ND):
                            last = t.matmul(
                                pg[bb][:],
                                wbuf[:, i * SLAB + hx * 128:i * SLAB + (hx + 1) * 128],
                                xn[i][:, tsl], start=(i == 0), stop=(i == ND - 1))
                        return last
                    mmt = p.emit("tensor", mm,
                                 waits=[xn_tok, pg_war] + ld_toks,
                                 inc=p.sem("s_pe"))
                    if hx == len(hcs) - 1:
                        for i in range(ND):
                            wbuf_last_use[i] = mmt
                    hsl = slice(hc * tc_width, (hc + 1) * tc_width)
                    if phase == "g":
                        def ev(s, bb=bb, hsl=hsl, pg=pg):
                            return s.activation(out=sg[:, hsl], in_=pg[bb][:], func=AF.Silu)
                        ev_hist.append(p.emit("scalar", ev, waits=[mmt], inc=p.sem("s_act")))
                        g_toks[hc] = ev_hist[-1]
                    else:
                        def ev(v, bb=bb, hsl=hsl, pg=pg):
                            return v.tensor_tensor(out=hb[:, hsl], in0=sg[:, hsl],
                                                   in1=pg[bb][:], op=ALU.mult)
                        ev_hist.append(p.emit("vector", ev,
                                              waits=[mmt, g_toks[hc]], inc=p.sem("s_dve")))
                    if phase == "u":
                        h_toks[hc] = ev_hist[-1]
        cmB.__exit__(None, None, None); cmA.__exit__(None, None, None)
        py, cms = [], []
        for d in range(ND):
            t_, cm_ = ctx.psum_scoped([128, tc_width], F32, "ffn_py")
            py.append(t_); cms.append(cm_)
        nview = (ND * SLAB) // 1024
        mm_last = None
        for hc in range(NH):
            vslot = hc % nview
            wv = [wbuf_last_use.get((vslot * 1024) // SLAB)]
            if hc >= nview:
                wv.append(mm_last)
            ldt = ctx.dma("sync", wbuf[:, vslot * 1024:(vslot + 1) * 1024],
                          wdT.ap()[hc * 128:(hc + 1) * 128, :], waits=wv)
            def mm(t, hc=hc, vslot=vslot, tsl=tsl, py=py):
                last = None
                for d in range(ND):
                    last = t.matmul(py[d][:],
                                    wbuf[:, vslot * 1024 + d * 128:vslot * 1024 + (d + 1) * 128],
                                    hb[:, hc * tc_width:(hc + 1) * tc_width],
                                    start=(hc == 0), stop=(hc == NH - 1))
                return last
            war = [ev_hist[-1], ev_hist[-2]] if hc == 0 else []
            mm_last = p.emit("tensor", mm, waits=[ldt, h_toks[hc]] + war, inc=p.sem("s_pe"))
        for i in range(ND):
            wbuf_last_use[i] = mm_last
        for d in range(ND):
            def ev(v, d=d, tsl=tsl, py=py):
                return v.tensor_tensor(out=x[d][:, tsl], in0=x[d][:, tsl],
                                       in1=py[d][:], op=ALU.add)
            done_toks.append(p.emit("vector", ev, waits=[mm_last], inc=p.sem("s_dve")))
        for cm_ in reversed(cms):
            cm_.__exit__(None, None, None)
    return tokens_max(done_toks[-ND:])



def emit_softmax_attn(ctx, consts, x, xn, xn_tok, wqkvT, fcT, qkv_b_ap, fc_b_ap,
                      head_mask_ap, w_tok, tag="at"):
    p = ctx.p
    HEADS, HD, GH = 16, 64, 4
    NG = HEADS // GH
    QH = 2
    QW = T // QH
    qb = ctx.pool(tag + "_qb", [128, 8], F32)
    kb = ctx.pool(tag + "_kb", [128, 8], F32)
    rows_bf = ctx.pool(tag + "_rows", [1, 2048], BF16)
    vb_row = rows_bf[:, 0:1024]
    fcb_row = rows_bf[:, 1024:2048]
    hm_col = ctx.pool(tag + "_hmc", [16, 1], F32)
    hm_row = ctx.pool(tag + "_hmr", [128, 16], F32)
    sv_toks = []
    for jf in range(8):
        for hh2 in range(2):
            h = jf * 2 + hh2
            sv_toks.append(ctx.dma("gpsimd", qb[hh2 * 64:(hh2 + 1) * 64, jf:jf + 1],
                                   qkv_b_ap[h * 192:h * 192 + 64].unsqueeze(1)))
            sv_toks.append(ctx.dma("gpsimd", kb[hh2 * 64:(hh2 + 1) * 64, jf:jf + 1],
                                   qkv_b_ap[h * 192 + 64:h * 192 + 128].unsqueeze(1)))
    vb_src = qkv_b_ap.rearrange("(h x) -> h x", x=192)[:, 128:192].unsqueeze(0)
    vb_dst = vb_row[:, :].rearrange("p (h x) -> p h x", x=64)
    sv_toks.append(ctx.dma("gpsimd", vb_dst, vb_src))
    sv_toks.append(ctx.dma("gpsimd", fcb_row[:], fc_b_ap[:].unsqueeze(0)))
    sv_toks.append(ctx.dma("gpsimd", hm_col[:], head_mask_ap[:].unsqueeze(1)))
    sv_toks.append(ctx.dma("gpsimd", hm_row[64:65, :], head_mask_ap[:].unsqueeze(0)))
    sv_tok = tokens_max(sv_toks)
    hm2 = ctx.pool(tag + "_hm2", [16, 1], F32)
    rhs16 = ctx.pool(tag + "_rhs16", [16, 8], F32)
    hmqcol = ctx.pool(tag + "_hmq", [128, 8], F32)
    headF = ctx.pool("c_headF", [16, 128], F32)
    headE = ctx.pool("c_headE", [16, 8], F32)
    sv2 = [ctx.dma("sync", headF[:], ctx.const_aps["headF"][:, :]),
           ctx.dma("sync", headE[:], ctx.const_aps["headE"][:, :])]
    t_h2a = p.emit("vector", lambda v: v.tensor_scalar(out=hm2[:], in0=hm_col[:],
                                                       scalar1=hm_col[:, 0:1],
                                                       scalar2=0.125, op0=ALU.mult,
                                                       op1=ALU.mult),
                   waits=[sv_tok] + sv2, inc=p.sem("s_dve"))
    t_hm2 = p.emit("vector", lambda v: v.tensor_scalar(out=rhs16[:], in0=headE[:],
                                                       scalar1=hm2[:, 0:1],
                                                       scalar2=None, op0=ALU.mult),
                   waits=[t_h2a], inc=p.sem("s_dve"))
    pHm, cmHm = ctx.psum_scoped([128, 8], F32, tag + "_phm")
    t_hmm = p.emit("tensor", lambda t: t.matmul(pHm[:], headF[:], rhs16[:],
                                                start=True, stop=True),
                   waits=[t_hm2], inc=p.sem("s_pe"))
    t_hmq = p.emit("vector", lambda v: v.tensor_copy(out=hmqcol[:], in_=pHm[:]),
                   waits=[t_hmm], inc=p.sem("s_dve"))
    cmHm.__exit__(None, None, None)

    qf = [ctx.pool(tag + f"_qf{i}", [128, T], BF16) for i in range(2)]
    kf = [ctx.pool(tag + f"_kf{i}", [128, T], BF16) for i in range(2)]
    vtk = [ctx.pool(tag + f"_vt{i}", [128, 260], BF16) for i in range(NT)]
    ao = [ctx.pool(tag + f"_ao{i}", [128, T], BF16) for i in range(ND)]
    pt = [ctx.pool(f"ln_xb{i}", [128, T // 2], BF16) for i in range(2)]
    rb = ctx.pool("sc_fA", [128, T // 2], F32)
    stg = ctx.pool(tag + "_stg", [64, QW], BF16)
    wbuf = ctx.pool("ffn_wbuf", [128, ND * 1024], BF16)
    nrm = ctx.pool(tag + "_nrm", [1, 32], F32)
    t_nrm0 = p.emit("vector", lambda v: v.memset(nrm[:], 0.0), inc=p.sem("s_dve"))
    mcol = ctx.pool(tag + "_mcol", [128, 1], F32)
    rrow = ctx.pool("sc_big", [128, T], F32)[0:1, :]
    qmx = ctx.pool(tag + "_qmx", [128, 1], F32)
    kmx = ctx.pool(tag + "_kmx", [128, 1], F32)

    x_done = []
    if ATTN_NG < NG:
        for a_ in range(ND):
            x_done.append(p.emit("vector", lambda v, a_=a_: v.memset(ao[a_][:], 0.0),
                                 inc=p.sem("s_dve")))
    prev_wbuf_reader = None
    prev_vtk_reader = None
    prev_pq_war = [t_hmq, t_hmq]
    for g in range(min(NG, ATTN_NG)):
        ld_toks = []
        for i in range(ND):
            for s_i, off in enumerate((g * 256, 1024 + g * 256, 2048 + g * 256)):
                wv = [w_tok, prev_wbuf_reader]
                ld_toks.append(ctx.dma("sync",
                                       wbuf[:, i * 768 + s_i * 256:i * 768 + (s_i + 1) * 256],
                                       wqkvT.ap()[i * 128:(i + 1) * 128, off:off + 256],
                                       waits=wv))
        pq0, cmq0 = ctx.psum_scoped([128, 512], F32, tag + "_pq")
        pq1, cmq1 = ctx.psum_scoped([128, 512], F32, tag + "_pq")
        pqs = [pq0, pq1]
        ev_hist = [prev_pq_war[0], prev_pq_war[1]]
        qk_evs = []
        for part in ("q", "k"):
            poff = 0 if part == "q" else 256
            dst = qf if part == "q" else kf
            for ft in range(2):
                jf_global = g * 2 + ft
                for tc in range(4):
                    bb = len(ev_hist) % 2
                    war = ev_hist[-2]
                    def mm(t, poff=poff, ft=ft, tc=tc, bb=bb):
                        last = None
                        for i in range(ND):
                            last = t.matmul(
                                pqs[bb][:],
                                wbuf[:, i * 768 + poff + ft * 128:i * 768 + poff + (ft + 1) * 128],
                                xn[i][:, tc * 512:(tc + 1) * 512],
                                start=(i == 0), stop=(i == ND - 1))
                        return last
                    mmt = p.emit("tensor", mm, waits=[xn_tok, war] + ld_toks,
                                 inc=p.sem("s_pe"))
                    if part == "q":
                        def ev(v, ft=ft, tc=tc, bb=bb, jf=jf_global):
                            return v.tensor_scalar(
                                out=qf[ft][:, tc * 512:(tc + 1) * 512], in0=pqs[bb][:],
                                scalar1=qb[:, jf:jf + 1], scalar2=hmqcol[:, jf:jf + 1],
                                op0=ALU.add, op1=ALU.mult)
                    else:
                        def ev(v, ft=ft, tc=tc, bb=bb, jf=jf_global):
                            return v.tensor_scalar(
                                out=kf[ft][:, tc * 512:(tc + 1) * 512], in0=pqs[bb][:],
                                scalar1=kb[:, jf:jf + 1], scalar2=None, op0=ALU.add)
                    ev_hist.append(p.emit("vector", ev, waits=[mmt, t_hmq],
                                          inc=p.sem("s_dve")))
                    qk_evs.append(ev_hist[-1])
        v_toks = []
        for tt in range(NT):
            vmt = p.emit("vector", lambda v, tt=tt: v.memset(vtk[tt][:], 1.0),
                         waits=[prev_vtk_reader], inc=p.sem("s_dve"))
            bb = len(ev_hist) % 2
            war = ev_hist[-2]
            def mmv(t, tt=tt, bb=bb, g=g):
                for i in range(ND):
                    t.matmul(pqs[bb][:, 0:256], xn[i][:, tt * 128:(tt + 1) * 128],
                             wbuf[:, i * 768 + 512:i * 768 + 768],
                             start=(i == 0), stop=False)
                return t.matmul(pqs[bb][:, 0:256], consts["ones_row"][:, 0:128],
                                vb_row[:, g * 256:(g + 1) * 256], start=False, stop=True)
            mmt = p.emit("tensor", mmv, waits=[xn_tok, war] + ld_toks, inc=p.sem("s_pe"))
            def evv(v, tt=tt, bb=bb):
                last = None
                for h4 in range(GH):
                    last = v.tensor_copy(out=vtk[tt][:, h4 * 65:h4 * 65 + 64],
                                         in_=pqs[bb][:, h4 * 64:(h4 + 1) * 64])
                return last
            ev_hist.append(p.emit("vector", evv, waits=[mmt, vmt], inc=p.sem("s_dve")))
            v_toks.append(ev_hist[-1])
            last_wbuf_mm = mmt
        cmq1.__exit__(None, None, None); cmq0.__exit__(None, None, None)
        qk_tok = tokens_max(qk_evs[-2:] + v_toks)
        prev_wbuf_reader = last_wbuf_mm

        pSa, cmSa = ctx.psum_scoped([128, QW], F32, tag + "_pSa")
        pSb, cmSb = ctx.psum_scoped([128, QW], F32, tag + "_pSb")
        pS2 = [pSa, pSb]
        pO, cmO = ctx.psum_scoped([65, QW], F32, tag + "_pO")
        pB, cmB = ctx.psum_scoped([128, QW], F32, tag + "_pB")
        grp_war = tokens_max(qk_evs[-2:])
        cells = rrow
        sq2 = pt
        prev_mm = {}
        red_toks = []
        for part in ("q", "k"):
            srcf = qf if part == "q" else kf
            cof = 0 if part == "q" else 32
            for ft in range(2):
                def sqr(s, ft=ft, srcf=srcf):
                    s.activation(out=sq2[0][:], in_=srcf[ft][:, 0:1024], func=AF.Square)
                    return s.activation(out=sq2[1][:], in_=srcf[ft][:, 1024:2048],
                                        func=AF.Square)
                w = [qk_tok] + list(prev_mm.values())
                tsq = p.emit("scalar", sqr, waits=w, inc=p.sem("s_act"))
                prev_mm = {}
                for hh2 in range(2):
                    h = g * GH + ft * 2 + hh2
                    par = hh2 * 64
                    def mmn(t, par=par):
                        last = None
                        for c in range(2):
                            t.matmul(pSa[0:1, c * 512:(c + 1) * 512],
                                     consts["ones_bf"][par:par + 64, :],
                                     sq2[0][par:par + 64, c * 512:(c + 1) * 512],
                                     start=True, stop=True)
                            last = t.matmul(pSb[0:1, c * 512:(c + 1) * 512],
                                            consts["ones_bf"][par:par + 64, :],
                                            sq2[1][par:par + 64, c * 512:(c + 1) * 512],
                                            start=True, stop=True)
                        return last
                    tn = p.emit("tensor", mmn,
                                waits=[tsq, grp_war] + red_toks[-2:],
                                inc=p.sem("s_pe"))
                    prev_mm[(part, ft, hh2)] = tn
                    def red(v, h=h, cof=cof):
                        v.tensor_reduce(out=cells[0:1, cof + h:cof + h + 1],
                                        in_=pSa[0:1, :], axis=AX.X, op=ALU.max)
                        return v.tensor_reduce(out=cells[0:1, cof + 16 + h:cof + 16 + h + 1],
                                               in_=pSb[0:1, :], axis=AX.X, op=ALU.max)
                    red_toks.append(p.emit("vector", red, waits=[tn], inc=p.sem("s_dve")))
        gsl = slice(g * GH, g * GH + GH)
        g0 = slice(gsl.start, gsl.stop)
        t_c1 = p.emit("vector", lambda v, g0=g0, gsl=gsl: v.tensor_tensor(
            out=cells[0:1, g0], in0=cells[0:1, g0],
            in1=cells[0:1, 16 + gsl.start:16 + gsl.stop], op=ALU.max),
            waits=red_toks[-2:] + [red_toks[-3]], inc=p.sem("s_dve"))
        t_c2 = p.emit("vector", lambda v, gsl=gsl: v.tensor_tensor(
            out=cells[0:1, 32 + gsl.start:32 + gsl.stop],
            in0=cells[0:1, 32 + gsl.start:32 + gsl.stop],
            in1=cells[0:1, 48 + gsl.start:48 + gsl.stop], op=ALU.max),
            waits=[t_c1], inc=p.sem("s_dve"))
        t_cmb = p.emit("vector", lambda v, g0=g0, gsl=gsl: v.tensor_tensor(
            out=cells[0:1, g0], in0=cells[0:1, g0],
            in1=cells[0:1, 32 + gsl.start:32 + gsl.stop], op=ALU.mult),
            waits=[t_c2], inc=p.sem("s_dve"))
        def msqrt(s, gsl=gsl):
            return s.activation(out=cells[0:1, gsl], in_=cells[0:1, gsl], func=AF.Sqrt)
        t_ms = p.emit("scalar", msqrt, waits=[t_cmb], inc=p.sem("s_act"))
        def mneg(v, gsl=gsl):
            return v.tensor_scalar(out=nrm[:, gsl], in0=cells[0:1, gsl],
                                   scalar1=-1.0, scalar2=None, op0=ALU.mult)
        t_nrm = p.emit("vector", mneg, waits=[t_ms, t_nrm0], inc=p.sem("s_dve"))

        exp_seq = []
        mmo_seq = []
        t_mc_prev_ext = None
        t_rb_prev = None
        rcp_prev = None
        nm_prev = None
        mv_prev = None
        for hh in ([] if BISECT_NO_HEADS else range(GH)):
            h = g * GH + hh
            ft, par = hh // 2, (hh % 2) * 64
            oft = h // 2
            def bcm(t, h=h):
                return t.matmul(pSa[:, 0:2], consts["ones_f"][0:1, :], nrm[:, h:h + 2],
                                start=True, stop=True)
            t_bm = p.emit("tensor", bcm,
                          waits=[t_nrm, exp_seq[-1] if exp_seq else None],
                          inc=p.sem("s_pe"))
            t_mc = p.emit("vector",
                          lambda v: v.tensor_copy(out=mcol[:], in_=pSa[:, 0:1]),
                          waits=[t_bm, t_mc_prev_ext], inc=p.sem("s_dve"))
            for qh in range(QH):
                qsl = slice(qh * QW, (qh + 1) * QW)
                for kt in range(NT):
                    bb = len(exp_seq) % 2
                    def mms(t, ft=ft, par=par, kt=kt, qh=qh, bb=bb):
                        last = None
                        for c in range(2):
                            last = t.matmul(
                                pS2[bb][:, c * 512:(c + 1) * 512],
                                kf[ft][par:par + 64, kt * 128:(kt + 1) * 128],
                                qf[ft][par:par + 64, qh * QW + c * 512:qh * QW + (c + 1) * 512],
                                start=True, stop=True)
                        return last
                    mmt = p.emit("tensor", mms,
                                 waits=[t_mc, qk_tok,
                                        exp_seq[-2] if len(exp_seq) >= 2 else None],
                                 inc=p.sem("s_pe"))
                    def ex(s, bb=bb):
                        return s.activation(out=pt[bb][:], in_=pS2[bb][:], func=AF.Exp,
                                            bias=mcol[:])
                    ext = p.emit("scalar", ex,
                                 waits=[mmt, t_mc,
                                        mmo_seq[-2] if len(mmo_seq) >= 2 else None],
                                 inc=p.sem("s_act"))
                    exp_seq.append(ext)
                    def mmo(t, kt=kt, hh=hh, bb=bb):
                        last = None
                        for c in range(2):
                            last = t.matmul(pO[:, c * 512:(c + 1) * 512],
                                            vtk[kt][:, hh * 65:(hh + 1) * 65],
                                            pt[bb][:, c * 512:(c + 1) * 512],
                                            start=(kt == 0), stop=(kt == NT - 1))
                        return last
                    mmo_seq.append(p.emit(
                        "tensor", mmo,
                        waits=[ext, v_toks[kt],
                               tokens_max([rcp_prev, nm_prev, mv_prev]) if kt == 0 else None],
                        inc=p.sem("s_pe")))
                t_mc_prev_ext = exp_seq[-1]
                t_r1 = p.emit("vector",
                              lambda v: v.reciprocal(out=rb[64:65, :], in_=pO[64:65, :]),
                              waits=[mmo_seq[-1], t_rb_prev], inc=p.sem("s_dve"))
                def rcp(v, h=h):
                    return v.tensor_scalar(out=rb[64:65, :], in0=rb[64:65, :],
                                           scalar1=hm_row[64:65, h:h + 1], scalar2=None,
                                           op0=ALU.mult)
                rcp_prev = p.emit("vector", rcp, waits=[t_r1], inc=p.sem("s_dve"))
                def bcr(t):
                    last = None
                    for c in range(2):
                        last = t.matmul(pB[:, c * 512:(c + 1) * 512],
                                        consts["ones_f"][64:65, :],
                                        rb[64:65, c * 512:(c + 1) * 512],
                                        start=True, stop=True)
                    return last
                t_br = p.emit("tensor", bcr, waits=[rcp_prev, t_rb_prev], inc=p.sem("s_pe"))
                t_rb = p.emit("vector", lambda v: v.tensor_copy(out=rb[:], in_=pB[:]),
                              waits=[t_br], inc=p.sem("s_dve"))
                t_rb_prev = t_rb
                redirect = ATTN_REDIRECT_G1 and g >= 1
                if par == 0:
                    def nm(v, oft=oft, qsl=qsl, redirect=redirect):
                        dst = stg[:] if redirect else ao[oft][0:64, qsl]
                        return v.tensor_tensor(out=dst, in0=rb[0:64, :],
                                               in1=pO[0:64, :], op=ALU.mult)
                    nm_prev = p.emit("vector", nm, waits=[t_rb, mv_prev], inc=p.sem("s_dve"))
                    x_done.append(nm_prev)
                else:
                    def nm(v):
                        return v.tensor_tensor(out=stg[:], in0=rb[0:64, :],
                                               in1=pO[0:64, :], op=ALU.mult)
                    nm_prev = p.emit("vector", nm, waits=[t_rb, mv_prev], inc=p.sem("s_dve"))
                    if redirect:
                        x_done.append(nm_prev)
                        mv_prev = None
                    else:
                        mv_prev = ctx.dma("gpsimd", ao[oft][64:128, qsl], stg[:],
                                          waits=[nm_prev])
                        x_done.append(mv_prev)
        prev_vtk_reader = mmo_seq[-1] if mmo_seq else t_nrm
        prev_pq_war = ([tokens_max([rcp_prev, nm_prev]), tokens_max([t_rb_prev])]
                       if not BISECT_NO_HEADS else [t_nrm, t_nrm])
        cmB.__exit__(None, None, None); cmO.__exit__(None, None, None)
        cmSb.__exit__(None, None, None); cmSa.__exit__(None, None, None)

    ao_tok = tokens_max(x_done[-(GH * QH):]) if x_done else prev_vtk_reader
    ld_toks = []
    for a in range(ND):
        ld_toks.append(ctx.dma("sync", wbuf[:, a * 1024:(a + 1) * 1024],
                               fcT.ap()[a * 128:(a + 1) * 128, :],
                               waits=[w_tok, prev_vtk_reader]))
    done = []
    py, cms = [], []
    for d in range(ND):
        t_, cm_ = ctx.psum_scoped([128, 512], F32, tag + "_py")
        py.append(t_); cms.append(cm_)
    prev_ev = [prev_pq_war[0]]
    for tc in range(4):
        tsl = slice(tc * 512, (tc + 1) * 512)
        mm_last = None
        for a in range(ND):
            def mm(t, a=a, tsl=tsl):
                last = None
                for d in range(ND):
                    last = t.matmul(py[d][:],
                                    wbuf[:, a * 1024 + d * 128:a * 1024 + (d + 1) * 128],
                                    ao[a][:, tsl], start=(a == 0), stop=False)
                return last
            w = [ld_toks[a], ao_tok] + (prev_ev if a == 0 else [])
            mm_last = p.emit("tensor", mm, waits=w, inc=p.sem("s_pe"))
        def mmb(t):
            last = None
            for d in range(ND):
                last = t.matmul(py[d][:], fcb_row[:, d * 128:(d + 1) * 128],
                                consts["ones_row"][:], start=False, stop=True)
            return last
        mm_last = p.emit("tensor", mmb, waits=[mm_last, sv_tok], inc=p.sem("s_pe"))
        evs = []
        for d in range(ND):
            def ev(v, d=d, tsl=tsl):
                return v.tensor_tensor(out=x[d][:, tsl], in0=x[d][:, tsl],
                                       in1=py[d][:], op=ALU.add)
            evs.append(p.emit("vector", ev, waits=[mm_last, xn_tok], inc=p.sem("s_dve")))
        prev_ev = evs
        done += evs
    for cm_ in reversed(cms):
        cm_.__exit__(None, None, None)
    return tokens_max(done[-ND:])



def emit_linattn(ctx, consts, x, xn, xn_tok, wqkv_bf, wout_bf, w_tok, tag="lb"):
    p = ctx.p
    HEADS, HD = 16, 64
    CH = 128
    NCH = T // CH
    wbuf = ctx.pool("ffn_wbuf", [128, ND * 1024], BF16)
    vtk = [ctx.pool(tag + "_vt0", [128, 1040], BF16)] * 2
    y = xn
    qp = [ctx.pool(tag + f"_qp{i}", [128, T], BF16) for i in range(ND)]
    kp = [ctx.pool(tag + f"_kp{i}", [128, T], BF16) for i in range(ND)]
    sc_big = ctx.pool("sc_big", [128, T], F32)
    raw = sc_big[:, 0:1024]
    rot = sc_big[:, 1024:2048]
    ebf = ctx.pool("ln_xb0", [128, T // 2], BF16)
    rpb = ctx.pool("ln_x2b0", [128, T // 2], BF16)
    cosT = ctx.pool(tag + "_cos", [128, T // 2], mybir.dt.float16)
    sinT = ctx.pool(tag + "_sin", [128, T // 2], mybir.dt.float16)
    ktb = ctx.pool("ln_xb1", [128, T // 2], BF16)
    pmb = ctx.pool(tag + "_pmb", [128, T // 2], BF16)

    last_v_mm = getattr(ctx, "_lb_wbuf_war", None)

    qk_toks = []
    pC, cmC = ctx.psum_scoped([128, T // 2], F32, tag + "_pc")
    prev_rope = {}
    last_c_mm = None
    for half in range(2):
        hs = slice(half * 1024, (half + 1) * 1024)
        tbl_toks = [ctx.dma("sync", cosT[:], ctx.const_aps["rope_cos"][:, hs],
                            waits=[prev_rope.get("tbl_user")]),
                    ctx.dma("sync", sinT[:], ctx.const_aps["rope_sin"][:, hs],
                            waits=[prev_rope.get("tbl_user")])]
        for part, dst, coff in (("q", qp, 0), ("k", kp, 1024)):
            ld_toks = []
            for i in range(ND):
                ld_toks.append(ctx.dma("sync", wbuf[:, i * 1024:(i + 1) * 1024],
                                       wqkv_bf.ap()[i * 128:(i + 1) * 128, coff:coff + 1024],
                                       waits=[w_tok, last_v_mm if (part == "q" and half == 0) else last_c_mm]))
            for oc in range(ND):
                def mmq(t, oc=oc, hs=hs):
                    last = None
                    for b4 in range(2):
                        for i in range(ND):
                            last = t.matmul(pC[:, b4 * 512:(b4 + 1) * 512],
                                            wbuf[:, i * 1024 + oc * 128:i * 1024 + (oc + 1) * 128],
                                            xn[i][:, half * 1024 + b4 * 512:half * 1024 + (b4 + 1) * 512],
                                            start=(i == 0), stop=(i == ND - 1))
                    return last
                mmt = p.emit("tensor", mmq,
                             waits=[xn_tok, prev_rope.get("last")] + ld_toks,
                             inc=p.sem("s_pe"))
                last_c_mm = mmt
                t_e = p.emit("scalar",
                             lambda s: s.activation(out=raw, in_=pC[:], func=AF.Copy),
                             waits=[mmt, prev_rope.get("done")], inc=p.sem("s_act"))
                rt = []
                for blk in range(2):
                    b0 = blk * 64
                    t_r1 = p.emit("vector",
                                  lambda v, b0=b0: v.tensor_copy(out=rot[b0:b0 + 32, :],
                                                                 in_=raw[b0 + 32:b0 + 64, :]),
                                  waits=[t_e, prev_rope.get("rot")], inc=p.sem("s_dve"))
                    t_r2 = p.emit("vector",
                                  lambda v, b0=b0: v.tensor_copy(out=rot[b0 + 32:b0 + 64, :],
                                                                 in_=raw[b0:b0 + 32, :]),
                                  waits=[t_e], inc=p.sem("s_dve"))
                    rt += [t_r1, t_r2]
                t_m1 = p.emit("vector",
                              lambda v: v.tensor_tensor(out=raw, in0=raw,
                                                        in1=cosT[:], op=ALU.mult),
                              waits=rt + tbl_toks, inc=p.sem("s_dve"))
                t_m2 = p.emit("vector",
                              lambda v: v.tensor_tensor(out=rot, in0=rot,
                                                        in1=sinT[:], op=ALU.mult),
                              waits=rt + tbl_toks, inc=p.sem("s_dve"))
                t_ad = p.emit("vector",
                              lambda v: v.tensor_tensor(out=raw, in0=raw,
                                                        in1=rot, op=ALU.add),
                              waits=[t_m1, t_m2], inc=p.sem("s_dve"))
                t_ex = p.emit("scalar",
                              lambda s: s.activation(out=ebf[:], in_=raw, func=AF.Exp),
                              waits=[t_ad, prev_rope.get("ebf")], inc=p.sem("s_act"))
                t_mn = p.emit("vector",
                              lambda v: v.tensor_scalar(out=ebf[:], in0=ebf[:], scalar1=1.0,
                                                        scalar2=None, op0=ALU.min),
                              waits=[t_ex], inc=p.sem("s_dve"))
                t_mx = p.emit("vector",
                              lambda v: v.tensor_scalar(out=rpb[:], in0=raw, scalar1=0.0,
                                                        scalar2=None, op0=ALU.max),
                              waits=[t_ad, prev_rope.get("rpb")], inc=p.sem("s_dve"))
                t_fin = p.emit("vector",
                               lambda v, dst=dst, oc=oc, hs=hs: v.tensor_tensor(
                                   out=dst[oc][:, hs], in0=ebf[:], in1=rpb[:], op=ALU.add),
                               waits=[t_mn, t_mx], inc=p.sem("s_dve"))
                prev_rope["done"] = tokens_max([t_mx, t_ex])
                prev_rope["rot"] = t_ad
                prev_rope["ebf"] = t_fin
                prev_rope["rpb"] = t_fin
                prev_rope["last"] = t_fin
                prev_rope["tbl_user"] = tokens_max([t_m1, t_m2])
                qk_toks.append(t_fin)
    cmC.__exit__(None, None, None)
    qk_tok = tokens_max(qk_toks[-2:])
    ld_v = []
    for i in range(ND):
        ld_v.append(ctx.dma("sync", wbuf[:, i * 1024:(i + 1) * 1024],
                            wqkv_bf.ap()[i * 128:(i + 1) * 128, 2048:3072],
                            waits=[w_tok, last_c_mm]))

    kvb_kv = ctx.pool("ln_x2b1", [128, T // 2], BF16)[:, 0:512]
    kvb_ks = ctx.pool(tag + "_kbks", [128, 8], BF16)
    pYe, cmYe = ctx.psum_scoped([65, 1024], F32, tag + "_pye")
    pYo, cmYo = ctx.psum_scoped([128, 1024], F32, tag + "_pyo")
    pKVS, cmKVS = ctx.psum_scoped([128, 1024], F32, tag + "_pkvs")
    sC = p.emit("vector", lambda v: v.memset(kvb_kv, 0.0), inc=p.sem("s_dve"))
    sD = p.emit("vector", lambda v: v.memset(kvb_ks[:], 0.0), waits=[sC], inc=p.sem("s_dve"))
    kv_ready = sD
    import os
    NCH_EFF = int(os.environ.get("LINCUT", NCH))
    LINB = int(os.environ.get("LINB", 99))
    prev = {}
    last_fin = []
    v_done = {}
    pT, cmT = ctx.psum_scoped([128, 1024], F32, tag + "_pt")
    pV = pS = pBc = pT
    for c in range(NCH_EFF):
        csl = slice(c * CH, (c + 1) * CH)
        vb = c % 2
        vmt = p.emit("vector", lambda v, vb=vb: v.memset(vtk[vb][:], 1.0),
                     waits=[prev.get("vthold")], inc=p.sem("s_dve"))
        def mmv(t, csl=csl):
            last = None
            for i in range(ND):
                for half in range(2):
                    last = t.matmul(pV[:, half * 512:(half + 1) * 512],
                                    xn[i][:, csl],
                                    wbuf[:, i * 1024 + half * 512:i * 1024 + (half + 1) * 512],
                                    start=(i == 0), stop=(i == ND - 1))
            return last
        t_vm = p.emit("tensor", mmv,
                      waits=[qk_tok, prev.get("vpsum"), prev.get("trans_free")] + ld_v,
                      inc=p.sem("s_pe"))
        def evv(v, vb=vb):
            dst = vtk[vb][:, :].rearrange("p (h c2) -> p h c2", c2=65)[:, :, 0:64]
            return v.tensor_copy(out=dst,
                                 in_=pV[:, :].rearrange("p (h c2) -> p h c2", c2=64))
        t_ve = p.emit("vector", evv, waits=[t_vm, vmt], inc=p.sem("s_dve"))
        v_done[c] = t_ve
        prev["vpsum"] = t_ve
        if LINB < 2:
            last_fin = [t_ve]
            prev["vthold"] = t_ve
            continue
        kt_toks = []
        for ft in range(ND):
            kt_toks.append(ctx.dma_tr("scalar", ktb[:, ft * 128:(ft + 1) * 128],
                                      kp[ft][:, csl],
                                      waits=[qk_tok, prev.get("ktb_war")]))
        t_ktb = tokens_max(kt_toks)
        if LINB < 3:
            last_fin = [t_ktb]
            prev["vthold"] = tokens_max([t_ve, t_ktb])
            prev["ktb_war"] = t_ktb
            continue
        t_y_groups = []
        pm_toks = []
        for hg in range(2):
            def mmS(t, csl=csl, hg=hg):
                t.drain()
                for hx in range(0, 8, 2):
                    h = hg * 8 + hx
                    ft = h // 2
                    t.matmul(pS[:, hx * 128:(hx + 1) * 128],
                             kp[ft][0:64, csl], qp[ft][0:64, csl],
                             start=True, stop=True)
                t.drain()
                for hx in range(1, 8, 2):
                    h = hg * 8 + hx
                    ft = h // 2
                    t.matmul(pS[:, hx * 128:(hx + 1) * 128],
                             kp[ft][64:128, csl], qp[ft][64:128, csl],
                             start=True, stop=True)
                return t.drain()
            if os.environ.get("LINS") == "nos":
                t_S = tokens_max([t_ktb, t_ve])
            else:
                t_S = p.emit("tensor", mmS,
                             waits=[qk_tok, prev.get(("s_free", hg)), t_ktb, t_ve,
                                    pm_toks[-1] if pm_toks else None],
                             inc=p.sem("s_pe"))
            def msk(v, hg=hg):
                return v.tensor_tensor(
                    out=pmb[:, :].rearrange("p (h t2) -> p h t2", t2=128),
                    in0=pS[:, :].rearrange("p (h t2) -> p h t2", t2=128),
                    in1=consts["triu_bf"][:, :].unsqueeze(1).broadcast_to((128, 8, 128)),
                    op=ALU.mult)
            if os.environ.get("LINS") == "nomask":
                t_pm = t_S
            else:
                t_pm = p.emit("vector", msk,
                              waits=[t_S, prev.get("pmb_war"), t_y_groups[-1] if t_y_groups else None],
                              inc=p.sem("s_dve"))
            pm_toks.append(t_pm)
            def mmy(t, c=c, csl=csl, hg=hg):
                for hx in range(8):
                    h = hg * 8 + hx
                    ft, par = h // 2, (h % 2) * 64
                    pos = (h // 2) * 128
                    pmo = hx * 128
                    h2 = h // 2
                    if h % 2 == 0:
                        t.matmul(pYe[0:64, pos:pos + 128],
                                 vtk[c % 2][:, h * 65:h * 65 + 64], pmb[:, pmo:pmo + 128],
                                 start=True, stop=False, skip_group_check=True)
                        t.matmul(pYe[0:64, pos:pos + 128],
                                 kvb_kv[0:64, h2 * 64:(h2 + 1) * 64],
                                 qp[ft][0:64, csl], start=False, stop=True,
                                 skip_group_check=True)
                        t.matmul(pYe[64:65, pos:pos + 128],
                                 vtk[c % 2][:, h * 65 + 64:h * 65 + 65], pmb[:, pmo:pmo + 128],
                                 start=True, stop=False, skip_group_check=True)
                        t.matmul(pYe[64:65, pos:pos + 128],
                                 kvb_ks[0:64, h2:h2 + 1],
                                 qp[ft][0:64, csl], start=False, stop=True,
                                 skip_group_check=True)
                    else:
                        t.matmul(pYo[64:128, pos:pos + 128],
                                 vtk[c % 2][:, h * 65:h * 65 + 64], pmb[:, pmo:pmo + 128],
                                 start=True, stop=False, skip_group_check=True)
                        t.matmul(pYo[0:1, pos:pos + 128],
                                 vtk[c % 2][:, h * 65 + 64:h * 65 + 65], pmb[:, pmo:pmo + 128],
                                 start=True, stop=False, skip_group_check=True)
                t.drain()
                last = None
                for hx in range(1, 8, 2):
                    h = hg * 8 + hx
                    ft = h // 2
                    pos = (h // 2) * 128
                    h2 = h // 2
                    t.matmul(pYo[64:128, pos:pos + 128],
                             kvb_kv[64:128, h2 * 64:(h2 + 1) * 64],
                             qp[ft][64:128, csl], start=False, stop=True,
                             skip_group_check=True)
                    last = t.matmul(pYo[0:1, pos:pos + 128],
                                    kvb_ks[64:128, h2:h2 + 1],
                                    qp[ft][64:128, csl], start=False, stop=True,
                                    skip_group_check=True)
                t.drain()
                return last
            if LINB < 4:
                t_yg = t_pm
            else:
                t_yg = p.emit("tensor", mmy,
                              waits=[t_pm, v_done[c], kv_ready, prev.get("y_war")],
                              inc=p.sem("s_pe"))
            t_y_groups.append(t_yg)
            prev[("s_free", hg)] = t_pm
        t_y = t_y_groups[-1]
        prev["pmb_war"] = t_y
        def mmu(t, c=c):
            last = None
            for h in range(HEADS):
                par, h2 = (h % 2) * 64, h // 2
                t.matmul(pKVS[par:par + 64, h2 * 128:h2 * 128 + 64],
                         ktb[:, h * 64:(h + 1) * 64], vtk[c % 2][:, h * 65:h * 65 + 64],
                         start=(c == 0), stop=(c == NCH_EFF - 1),
                         skip_group_check=True)
                last = t.matmul(pKVS[par:par + 64, h2 * 128 + 64:h2 * 128 + 65],
                                ktb[:, h * 64:(h + 1) * 64], consts["ones_bf"][:],
                                start=(c == 0), stop=(c == NCH_EFF - 1),
                                skip_group_check=True)
            return last
        if LINB < 5:
            last_fin = [t_y]
            prev["vthold"] = tokens_max([t_ve, t_y])
            prev["ktb_war"] = t_ktb
            prev["pmb_war"] = t_y
            prev["y_war"] = t_y
            continue
        t_u = p.emit("tensor", mmu, waits=[t_ktb, v_done[c], prev.get("kvb_done")],
                     inc=p.sem("s_pe"))
        prev["ktb_war"] = t_u
        t_c1 = p.emit("vector", lambda v: v.tensor_copy(
                          out=kvb_kv.rearrange("p (h c2) -> p h c2", c2=64),
                          in_=pKVS[:, :].rearrange("p (h c2) -> p h c2", c2=128)[:, :, 0:64]),
                      waits=[t_u, t_y], inc=p.sem("s_dve"))
        t_c2 = p.emit("vector", lambda v: v.tensor_copy(
                          out=kvb_ks[:, :].rearrange("p (h c2) -> p h c2", c2=1),
                          in_=pKVS[:, :].rearrange("p (h c2) -> p h c2", c2=128)[:, :, 64:65]),
                      waits=[t_c1], inc=p.sem("s_dve"))
        kv_ready = t_c2
        prev["kvb_done"] = t_c2
        if LINB < 6:
            last_fin = [kv_ready, t_y]
            prev["vthold"] = tokens_max([t_ve, t_y, t_u])
            prev["ktb_war"] = t_u
            prev["pmb_war"] = t_y
            prev["y_war"] = t_y
            continue
        t_dme = p.emit("vector",
                       lambda v: v.tensor_scalar(out=raw[64:65, :], in0=pYe[64:65, :],
                                                 scalar1=1e-6, scalar2=None, op0=ALU.max),
                       waits=[t_y, prev.get("rawrow_war")], inc=p.sem("s_dve"))
        t_rce = p.emit("vector",
                       lambda v: v.reciprocal(out=rot[64:65, :], in_=raw[64:65, :]),
                       waits=[t_dme, prev.get("rotrow_war")], inc=p.sem("s_dve"))
        t_dmo = p.emit("vector",
                       lambda v: v.tensor_scalar(out=raw[0:1, :], in0=pYo[0:1, :],
                                                 scalar1=1e-6, scalar2=None, op0=ALU.max),
                       waits=[t_y, prev.get("rawrow_war")], inc=p.sem("s_dve"))
        t_rco = p.emit("vector",
                       lambda v: v.reciprocal(out=rot[0:1, :], in_=raw[0:1, :]),
                       waits=[t_dmo, prev.get("rotrow_war")], inc=p.sem("s_dve"))
        def bce(t):
            last = None
            for cc in range(2):
                last = t.matmul(pBc[:, cc * 512:(cc + 1) * 512],
                                consts["ones_f"][64:65, :],
                                rot[64:65, cc * 512:(cc + 1) * 512], start=True, stop=True)
            return last
        t_be = p.emit("tensor", bce, waits=[t_rce, prev.get("bc_war"), pm_toks[-1]],
                      inc=p.sem("s_pe"))
        t_rbe = p.emit("vector",
                       lambda v: v.tensor_copy(out=sc_big[0:64, 0:1024], in_=pBc[0:64, :]),
                       waits=[t_be, prev.get("rawfull_war")], inc=p.sem("s_dve"))
        def bco(t):
            last = None
            for cc in range(2):
                last = t.matmul(pBc[:, cc * 512:(cc + 1) * 512],
                                consts["ones_f"][0:1, :],
                                rot[0:1, cc * 512:(cc + 1) * 512], start=True, stop=True)
            return last
        t_bo = p.emit("tensor", bco, waits=[t_rco, t_rbe], inc=p.sem("s_pe"))
        t_rbc = p.emit("vector",
                       lambda v: v.tensor_copy(out=sc_big[64:128, 0:1024], in_=pBc[64:128, :]),
                       waits=[t_bo], inc=p.sem("s_dve"))
        fins = []
        for j in range(8):
            def fe(v, j=j, csl=csl):
                return v.tensor_tensor(out=y[j][0:64, csl],
                                       in0=sc_big[0:64, j * 128:(j + 1) * 128],
                                       in1=pYe[0:64, j * 128:(j + 1) * 128], op=ALU.mult)
            fins.append(p.emit("vector", fe, waits=[t_rbe, t_rbc], inc=p.sem("s_dve")))
            def fo(v, j=j, csl=csl):
                return v.tensor_tensor(out=y[j][64:128, csl],
                                       in0=sc_big[64:128, j * 128:(j + 1) * 128],
                                       in1=pYo[64:128, j * 128:(j + 1) * 128], op=ALU.mult)
            fins.append(p.emit("vector", fo, waits=[t_rbc], inc=p.sem("s_dve")))
        last_fin = fins
        prev["trans_free"] = tokens_max([t_rbc])
        prev["y_war"] = tokens_max(fins[-2:] + [t_dme, t_dmo])
        prev["rawrow_war"] = tokens_max([t_rce, t_rco, t_rbc] + fins[-2:])
        prev["rotrow_war"] = tokens_max([t_be, t_bo])
        prev["bc_war"] = t_rbc
        prev["rawfull_war"] = tokens_max(fins[-2:])
        prev["vthold"] = tokens_max(fins[-2:] + [t_u])
    cmT.__exit__(None, None, None)
    cmKVS.__exit__(None, None, None)
    cmYo.__exit__(None, None, None); cmYe.__exit__(None, None, None)
    y_tok = tokens_max(last_fin) if last_fin else qk_tok

    ld_toks = []
    for i in range(ND):
        ld_toks.append(ctx.dma("sync", wbuf[:, i * 1024:(i + 1) * 1024],
                               wout_bf.ap()[i * 128:(i + 1) * 128, :],
                               waits=[w_tok, prev.get("ktb_war"), y_tok]))
    py, cms = [], []
    for d in range(ND):
        t_, cm_ = ctx.psum_scoped([128, 512], F32, tag + "_pyy")
        py.append(t_); cms.append(cm_)
    done = []
    prev_ev = [prev.get("trans_free"), kv_ready]
    for tc in range(4):
        tsl = slice(tc * 512, (tc + 1) * 512)
        mm_last = None
        for a in range(ND):
            def mm(t, a=a, tsl=tsl, py=py):
                last = None
                for d in range(ND):
                    last = t.matmul(py[d][:],
                                    wbuf[:, a * 1024 + d * 128:a * 1024 + (d + 1) * 128],
                                    y[a][:, tsl], start=(a == 0), stop=(a == ND - 1))
                return last
            w = [ld_toks[a], y_tok] + (prev_ev if a == 0 else [])
            mm_last = p.emit("tensor", mm, waits=w, inc=p.sem("s_pe"))
        evs = []
        for d in range(ND):
            def ev(v, d=d, tsl=tsl, py=py):
                return v.tensor_tensor(out=x[d][:, tsl], in0=x[d][:, tsl],
                                       in1=py[d][:], op=ALU.add)
            evs.append(p.emit("vector", ev, waits=[mm_last], inc=p.sem("s_dve")))
        prev_ev = evs
        done += evs
    for cm_ in reversed(cms):
        cm_.__exit__(None, None, None)
    ctx._lb_wbuf_war = mm_last
    return tokens_max(done[-ND:])



def emit_mamba_prep(ctx, ins):
    bigT, t1 = emit_prep_transposed(ctx, ins["in_proj_w"][0:4224, :], 4224, 1024,
                                    name="mbigT", bounce="mam_zsil")
    tailT, t2 = emit_prep_transposed(ctx, ins["in_proj_w"][4128:4256, :], 128, 1024,
                                     name="mtailT", bounce="mam_zsil")
    opT, t3 = emit_prep_transposed(ctx, ins["out_proj_w"], 1024, 2048, name="mopT",
                                   bounce="mam_zsil")
    fc1T, t4 = emit_prep_transposed(ctx, ins["fc1_w"], 8192, 1024, name="mfc1T",
                                    bounce="mam_zsil")
    fc2T, t5 = emit_prep_transposed(ctx, ins["fc2_w"], 1024, 4096, name="mfc2T",
                                    bounce="mam_zsil")
    return (dict(bigT=bigT, tailT=tailT, opT=opT, fc1T=fc1T, fc2T=fc2T,
                 tailT_tok=t2),
            tokens_max([t3, t4, t5]))


def emit_mamba(ctx, consts, x, xn, xn_tok, W, w_tok, tag="mb"):
    import os
    p = ctx.p
    NB = int(os.environ.get("MB_NB", 4))
    NCHB = int(os.environ.get("MB_NCH", 4))
    TB, CH, NH = 512, 128, 32
    bigT, tailT, opT = W["bigT"], W["tailT"], W["opT"]
    fc1T, fc2T = W["fc1T"], W["fc2T"]

    wbuf = W.get("wbuf")
    if wbuf is None:
        wbuf = ctx.pool("mam_wbuf", [128, 4096], BF16)
    arena = ctx.pool("mam_arena", [128, 9288], BF16)
    zsil = ctx.pool("mam_zsil", [128, ND * 1024], BF16)
    ind_sb = ctx.pool("mam_ind", [32, 4096], BF16)
    stash = ctx.pool("mam_stash", [128, 54], BF16)
    xst = ctx.pool("mam_xst", [128, 2048], BF16)
    dxb = ctx.pool("mam_dtxB", [128, 2048], BF16)
    gx = [ctx.pool("ln_xb0", [128, T // 2], BF16), ctx.pool("ln_xb1", [128, T // 2], BF16)]
    cbp = [ctx.pool("ln_x2b0", [128, T // 2], BF16), ctx.pool("ln_x2b1", [128, T // 2], BF16)]
    msb = ctx.pool("mam_ms", [128, 128], BF16)
    rhs64 = ctx.pool("mam_rhs", [64, 2048], F32)
    lhsT64 = ctx.pool("mam_lhs", [64, 128], F32)
    sel = ctx.pool("mam_sel", [64, 64], F32)
    S_sb = ctx.pool("mam_ssb", [64, 2048], F32)
    S_bf = ctx.pool("mam_sbf", [64, 2048], BF16)
    asm = ctx.pool("mam_asm", [64, 32], F32)
    arow_sb = ctx.pool("mam_arow", [1, 32], F32)
    acol = ctx.pool("mam_acol", [32, 1], F32)
    pk = ctx.pool("mam_pk", [128, 128], BF16)
    pkt = ctx.pool("mam_pkt", [128, 128], BF16)
    btk = ctx.pool("mam_btk", [128, 128], BF16)
    rstg = ctx.pool("mam_rstg", [32, 128], BF16)
    tail_sb = ctx.pool("mam_tail", [128, 256], BF16)
    cw = ctx.pool("mam_cw", [128, 72], F32)
    cb = ctx.pool("mam_cb", [128, 18], F32)
    dtb = ctx.pool("mam_dtb", [32, 1], F32)
    negA = ctx.pool("mam_negA", [32, 1], F32)
    d128 = ctx.pool("mam_d128", [128, 16], F32)
    ssm16 = ctx.pool("mam_ssm", [128, 16], F32)
    rw8 = ctx.pool("mam_rw8", [128, 8], F32)
    om2k = ctx.pool("mam_om2k", [128, 1], BF16)
    dev = ctx.pool("mam_dev", [1, 32], F32)
    sc_big = ctx.pool("sc_big", [128, T], F32)
    sc_fA = ctx.pool("sc_fA", [128, T // 2], F32)
    dt0 = sc_big[0:32, 0:512]
    sp1 = sc_big[0:32, 512:1024]
    dt1 = sc_big[0:32, 1024:1536]
    Lc = sc_big[0:32, 1536:2048]
    rtmp = sc_fA[0:32, 256:384]
    lneg = sc_fA[0:32, 384:512]
    cvs = sc_fA[:, 512:1024]
    rowA = sc_fA[0:1, 0:128]
    eps1, eps1_tok = ctx.constv(1e-5, 1)
    one32, one32_tok = ctx.constv(1.0, 32)

    pD, cmD = ctx.psum_scoped([128, 2048], F32, tag + "_pd")
    pY, cmY = ctx.psum_scoped([128, 2048], F32, tag + "_py")

    sv = []
    for j in range(16):
        sv.append(ctx.dma("sync", cw[:, 4 * j:4 * j + 4],
                          W["conv_w"][j * 128:(j + 1) * 128, :]))
    sv.append(ctx.dma("sync", cw[0:64, 64:68], W["conv_w"][2048:2112, :]))
    sv.append(ctx.dma("sync", cw[0:64, 68:72], W["conv_w"][2112:2176, :]))
    for j in range(16):
        sv.append(ctx.dma("sync", cb[:, j:j + 1],
                          W["conv_b"][j * 128:(j + 1) * 128].unsqueeze(1)))
    sv.append(ctx.dma("sync", cb[0:64, 16:17], W["conv_b"][2048:2112].unsqueeze(1)))
    sv.append(ctx.dma("sync", cb[0:64, 17:18], W["conv_b"][2112:2176].unsqueeze(1)))
    t_dtb = ctx.dma("sync", dtb[:], W["dt_bias"][:].unsqueeze(1))
    t_alog = ctx.dma("sync", negA[:], W["A_log"][:].unsqueeze(1))
    for j in range(16):
        sv.append(ctx.dma("sync", ssm16[:, j:j + 1],
                          W["ssm_norm_w"][j * 128:(j + 1) * 128].unsqueeze(1)))
    for j in range(8):
        sv.append(ctx.dma("sync", rw8[:, j:j + 1],
                          W["rms_w"][j * 128:(j + 1) * 128].unsqueeze(1)))
    t_dev = ctx.dma("sync", dev[0:1, 0:16], W["D"][0::2].unsqueeze(0),
                    allow_slow_non_contiguous=True)
    t_dod = ctx.dma("sync", dev[0:1, 16:32], W["D"][1::2].unsqueeze(0),
                    allow_slow_non_contiguous=True)
    t_tail = ctx.dma("sync", tail_sb[:, :].rearrange("p (i c) -> p i c", c=32),
                     tailT.ap()[:, 96:128].rearrange("(i p) c -> p i c", p=128),
                     waits=[W["tailT_tok"]])
    t_indc = ctx.dma("gpsimd", ind_sb[:], ctx.const_aps["indh"][:, :])
    t_na1 = p.emit("scalar", lambda s: s.activation(out=negA[:], in_=negA[:],
                                                    func=AF.Exp),
                   waits=[t_alog], inc=p.sem("s_act"))
    t_na = p.emit("vector", lambda v: v.tensor_scalar(out=negA[:], in0=negA[:],
                                                      scalar1=-1.0, scalar2=None,
                                                      op0=ALU.mult),
                  waits=[t_na1], inc=p.sem("s_dve"))
    def mmdd(t):
        t.matmul(pD[0:64, 0:16], consts["ones_f"][0:1, 0:64], dev[0:1, 0:16],
                 start=True, stop=True)
        return t.matmul(pD[64:128, 0:16], consts["ones_f"][0:1, 0:64],
                        dev[0:1, 16:32], start=True, stop=True)
    t_dmm = p.emit("tensor", mmdd, waits=[t_dev, t_dod, consts["ready"], xn_tok],
                   inc=p.sem("s_pe"))
    t_d128 = p.emit("scalar", lambda s: s.activation(out=d128[:], in_=pD[:, 0:16],
                                                     func=AF.Copy),
                    waits=[t_dmm], inc=p.sem("s_act"))

    def mset(v):
        v.memset(om2k[:], 1.0 / 2048.0)
        v.memset(S_sb[:], 0.0)
        v.memset(S_bf[:], 0.0)
        v.memset(rhs64[0:32, :], 0.0)
        v.memset(lhsT64[0:32, :], 1.0)
        v.memset(sel[0:32, :], 1.0)
        v.memset(sel[32:64, :], 0.0)
        v.memset(pk[64:128, :], 0.0)
        return v.memset(arena[64:128, 16 * 516:9288], 0.0)
    t_ms = p.emit("vector", mset, inc=p.sem("s_dve"))
    sv_tok = tokens_max(sv + [t_na, t_d128, t_ms, t_dtb, t_tail, t_indc])

    pv = {"pD_free": t_d128, "S_bf_tok": t_ms}
    NXT = 18

    def ext_ap(j, c0, c1, rows=128):
        return arena[0:rows, j * 516 + c0:j * 516 + c1]

    def xsf(j, csl, rows=128):
        return arena[0:rows, j * 516 + 3 + csl.start:j * 516 + 3 + csl.stop]

    def xs_chunk(csl):
        return arena[:, 0:16 * 516].rearrange("p (j c) -> p j c", c=516)[
            :, :, 3 + csl.start:3 + csl.stop]

    zs_view = zsil[:, :].rearrange("p (j c) -> p j c", c=512)
    yr_view = zs_view

    for b in range(NB):
        bsl = slice(b * TB, (b + 1) * TB)
        if b == 0:
            def hz(v):
                last = None
                for j in range(NXT):
                    last = v.memset(ext_ap(j, 0, 3), 0.0)
                return last
            t_hist = p.emit("vector", hz, waits=[t_ms], inc=p.sem("s_dve"))
        else:
            def hc(v):
                last = None
                for j in range(NXT):
                    last = v.tensor_copy(out=ext_ap(j, 0, 3),
                                         in_=stash[:, j * 3:(j + 1) * 3])
                return last
            t_hist = p.emit("vector", hc, waits=[pv.get("conv_done")],
                            inc=p.sem("s_dve"))
        for j in range(16):
            wb = j % 2
            ws = wb * 1024
            ld = ctx.dma("sync",
                         wbuf[:, ws:ws + 1024].rearrange("p (i c) -> p i c", c=128),
                         bigT.ap()[:, j * 128:(j + 1) * 128].rearrange(
                             "(i p) c -> p i c", p=128),
                         waits=[w_tok, pv.get(("wz", wb))])
            psl = slice(wb * 512, wb * 512 + 512)
            def mmz(t, ws=ws, psl=psl, bsl=bsl):
                last = None
                for i in range(ND):
                    last = t.matmul(pD[:, psl], wbuf[:, ws + i * 128:ws + (i + 1) * 128],
                                    xn[i][:, bsl], start=(i == 0), stop=(i == ND - 1))
                return last
            t_mm = p.emit("tensor", mmz,
                          waits=[ld, xn_tok, pv.get(("pp", wb)), pv.get("pD_free")],
                          inc=p.sem("s_pe"))
            pv[("wz", wb)] = t_mm
            def evz(s, j=j, psl=psl):
                return s.activation(out=zsil[:, j * 512:(j + 1) * 512],
                                    in_=pD[:, psl], func=AF.Silu)
            t_ev = p.emit("scalar", evz, waits=[t_mm, pv.get("zsil_war")],
                          inc=p.sem("s_act"))
            pv[("pp", wb)] = t_ev
        for j in range(NXT):
            wb = j % 2
            ws = wb * 1024
            if j < 16:
                coff, cwid = 2048 + j * 128, 128
            elif j == 16:
                coff, cwid = 4096, 64
            else:
                coff, cwid = 4160, 64
            ld = ctx.dma("sync",
                         wbuf[:, ws:ws + 8 * cwid].rearrange("p (i c) -> p i c", c=cwid),
                         bigT.ap()[:, coff:coff + cwid].rearrange(
                             "(i p) c -> p i c", p=128),
                         waits=[w_tok, pv.get(("wz", wb))])
            psl = slice(wb * 512, wb * 512 + 512)
            rows = 128 if j < 16 else 64
            def mmx(t, ws=ws, psl=psl, bsl=bsl, cwid=cwid, rows=rows):
                last = None
                for i in range(ND):
                    last = t.matmul(pD[0:rows, psl],
                                    wbuf[:, ws + i * cwid:ws + (i + 1) * cwid],
                                    xn[i][:, bsl], start=(i == 0), stop=(i == ND - 1))
                return last
            t_mm = p.emit("tensor", mmx, waits=[ld, xn_tok, pv.get(("pp", wb))],
                          inc=p.sem("s_pe"))
            pv[("wz", wb)] = t_mm
            def evx(s, j=j, psl=psl, rows=rows):
                return s.activation(out=ext_ap(j, 3, 515, rows), in_=pD[0:rows, psl],
                                    func=AF.Copy)
            t_ev = p.emit("scalar", evx, waits=[t_mm, t_hist, pv.get("ext_war")],
                          inc=p.sem("s_act"))
            pv[("pp", wb)] = t_ev
            pv[("ext", j)] = t_ev
        def mmdt(t, bsl=bsl):
            last = None
            for i in range(ND):
                last = t.matmul(pD[0:32, 0:512],
                                tail_sb[:, i * 32:(i + 1) * 32],
                                xn[i][:, bsl], start=(i == 0), stop=(i == ND - 1))
            return last
        t_mdt = p.emit("tensor", mmdt,
                       waits=[t_tail, xn_tok, pv.get(("pp", 0)), pv.get(("pp", 1))],
                       inc=p.sem("s_pe"))
        t_dt0 = p.emit("vector",
                       lambda v: v.tensor_scalar(out=dt0, in0=pD[0:32, 0:512],
                                                 scalar1=dtb[:], scalar2=None,
                                                 op0=ALU.add),
                       waits=[t_mdt, t_dtb, pv.get("dtrow_war")], inc=p.sem("s_dve"))
        pv["pD_sfree"] = tokens_max([t_dt0, pv[("pp", 0)], pv[("pp", 1)],
                                     flat(pv.get("pD_sfree"))])
        pv["pD_dfree"] = pv["pD_sfree"]
        def stc(v):
            last = None
            for j in range(NXT):
                last = v.tensor_copy(out=stash[:, j * 3:(j + 1) * 3],
                                     in_=ext_ap(j, 512, 515))
            return last
        t_stash = p.emit("vector", stc,
                         waits=[tokens_max([pv[("ext", j)] for j in range(NXT)])],
                         inc=p.sem("s_dve"))
        conv_done = []
        cvs2 = pD[:, 1024:1536]
        prev_cvs = pv.get("cvs_war")
        for j in range(NXT):
            rows = 128 if j < 16 else 64
            t0 = p.emit("vector",
                        lambda v, j=j, rows=rows: v.tensor_scalar(
                            out=cvs[0:rows, :] if rows == 128 else cvs[0:64, :],
                            in0=ext_ap(j, 0, 512, rows),
                            scalar1=cw[0:rows, 4 * j:4 * j + 1], scalar2=None,
                            op0=ALU.mult),
                        waits=[pv[("ext", j)], t_hist, sv_tok, prev_cvs],
                        inc=p.sem("s_dve"))
            t1 = p.emit("vector",
                        lambda v, j=j, rows=rows: v.tensor_scalar(
                            out=cvs2[0:rows, :], in0=ext_ap(j, 1, 513, rows),
                            scalar1=cw[0:rows, 4 * j + 1:4 * j + 2], scalar2=None,
                            op0=ALU.mult),
                        waits=[pv[("ext", j)], prev_cvs], inc=p.sem("s_dve"))
            t2 = p.emit("vector",
                        lambda v, rows=rows: v.tensor_tensor(
                            out=cvs[0:rows, :], in0=cvs[0:rows, :], in1=cvs2[0:rows, :],
                            op=ALU.add),
                        waits=[t0, t1], inc=p.sem("s_dve"))
            t3 = p.emit("vector",
                        lambda v, j=j, rows=rows: v.tensor_scalar(
                            out=cvs2[0:rows, :], in0=ext_ap(j, 2, 514, rows),
                            scalar1=cw[0:rows, 4 * j + 2:4 * j + 3], scalar2=None,
                            op0=ALU.mult),
                        waits=[t2], inc=p.sem("s_dve"))
            t4 = p.emit("vector",
                        lambda v, rows=rows: v.tensor_tensor(
                            out=cvs[0:rows, :], in0=cvs[0:rows, :], in1=cvs2[0:rows, :],
                            op=ALU.add),
                        waits=[t3], inc=p.sem("s_dve"))
            t5 = p.emit("vector",
                        lambda v, j=j, rows=rows: v.tensor_scalar(
                            out=cvs2[0:rows, :], in0=ext_ap(j, 3, 515, rows),
                            scalar1=cw[0:rows, 4 * j + 3:4 * j + 4], scalar2=None,
                            op0=ALU.mult),
                        waits=[t4], inc=p.sem("s_dve"))
            t6 = p.emit("vector",
                        lambda v, rows=rows: v.tensor_tensor(
                            out=cvs[0:rows, :], in0=cvs[0:rows, :], in1=cvs2[0:rows, :],
                            op=ALU.add),
                        waits=[t5], inc=p.sem("s_dve"))
            t7 = p.emit("scalar",
                        lambda s, j=j, rows=rows: s.activation(
                            out=xsf(j, slice(0, 512), rows), in_=cvs[0:rows, :],
                            func=AF.Silu, bias=cb[0:rows, j:j + 1]),
                        waits=[t6, sv_tok, t_stash, pv.get("xsf_war")],
                        inc=p.sem("s_act"))
            prev_cvs = t7
            conv_done.append(t7)
            pv[("xsf", j)] = t7
        pv["cvs_war"] = prev_cvs
        pv["conv_done"] = tokens_max(conv_done)
        t_a = p.emit("scalar", lambda s: s.activation(out=sp1, in_=dt0, func=AF.Abs),
                     waits=[t_dt0, pv.get("sp1_war")], inc=p.sem("s_act"))
        t_e = p.emit("scalar", lambda s: s.activation(out=sp1, in_=sp1, func=AF.Exp,
                                                      scale=-1.0),
                     waits=[t_a], inc=p.sem("s_act"))
        t_l = p.emit("scalar", lambda s: s.activation(out=sp1, in_=sp1, func=AF.Ln,
                                                      bias=one32[:]),
                     waits=[t_e, one32_tok], inc=p.sem("s_act"))
        t_m = p.emit("vector", lambda v: v.tensor_scalar(out=dt1, in0=dt0, scalar1=0.0,
                                                         scalar2=None, op0=ALU.max),
                     waits=[t_dt0, pv.get("dt1_war")], inc=p.sem("s_dve"))
        t_d1 = p.emit("vector", lambda v: v.tensor_tensor(out=dt1, in0=sp1, in1=dt1,
                                                          op=ALU.add),
                      waits=[t_l, t_m], inc=p.sem("s_dve"))
        t_g = p.emit("vector", lambda v: v.tensor_scalar(out=dt0, in0=dt1,
                                                         scalar1=negA[:], scalar2=None,
                                                         op0=ALU.mult),
                     waits=[t_d1, t_na], inc=p.sem("s_dve"))
        def scans(v):
            last = None
            for q in range(NCHB):
                qs = slice(q * 128, (q + 1) * 128)
                last = v.tensor_tensor_scan(out=Lc[:, qs],
                                            data0=consts["ones_f"][0:32, 0:128],
                                            data1=dt0[:, qs], initial=0.0,
                                            op0=ALU.mult, op1=ALU.add)
            return last
        t_Lq = p.emit("vector", scans, waits=[t_g, t_ms, pv.get("Lc_war")],
                      inc=p.sem("s_dve"))
        for q in range(NCHB):
            c = b * NCHB + q
            csl = slice(q * 128, (q + 1) * 128)
            cend = q * 128 + 127
            t_ln_ = p.emit("vector",
                           lambda v, csl=csl: v.tensor_scalar(
                               out=lneg, in0=Lc[:, csl], scalar1=-1.0, scalar2=None,
                               op0=ALU.mult),
                           waits=[t_Lq, pv.get("lneg_war")], inc=p.sem("s_dve"))
            t_ldma = ctx.dma("scalar", lhsT64[32:64, :], lneg,
                             waits=[t_ln_, pv.get("lhs_war")])
            t_pk1 = p.emit("scalar",
                           lambda s, csl=csl: s.activation(out=pk[0:32, :],
                                                           in_=dt1[:, csl],
                                                           func=AF.Copy),
                           waits=[t_d1, pv.get("pk_war")], inc=p.sem("s_act"))
            t_rr = p.emit("vector",
                          lambda v, csl=csl, cend=cend: v.tensor_scalar(
                              out=rtmp, in0=Lc[:, csl],
                              scalar1=Lc[0:32, cend:cend + 1], scalar2=-1.0,
                              op0=ALU.subtract, op1=ALU.mult),
                          waits=[t_Lq, pv.get("rtmp_war")], inc=p.sem("s_dve"))
            t_rex = p.emit("scalar",
                           lambda s: s.activation(out=rstg[:], in_=rtmp, func=AF.Exp),
                           waits=[t_rr, pv.get("rstg_war")], inc=p.sem("s_act"))
            t_rdma = ctx.dma("scalar", pk[32:64, :], rstg[:],
                             waits=[t_rex, pv.get("pk_war")])
            t_pkt = ctx.dma_tr("scalar", pkt[:], pk[:],
                               waits=[t_pk1, t_rdma, pv.get("pkt_war")])
            pv["pk_war"] = t_pkt
            tr_toks = []
            for ft in range(16):
                tr_toks.append(ctx.dma_tr("scalar", xst[:, ft * 128:(ft + 1) * 128],
                                          xsf(ft, csl),
                                          waits=[pv[("xsf", ft)], pv.get("xst_war")]))
            t_bt = ctx.dma_tr("scalar", btk[:], xsf(16, csl),
                              waits=[pv[("xsf", 16)], pv.get("btk_war")])
            v128 = (128, 32, 64)
            t_dxa = p.emit("vector",
                           lambda v: v.tensor_tensor(
                               out=xst[:, :].rearrange("p (h d) -> p h d", d=64),
                               in0=xst[:, :].rearrange("p (h d) -> p h d", d=64),
                               in1=pkt[:, 0:32].unsqueeze(2).broadcast_to(v128),
                               op=ALU.mult),
                           waits=[t_pkt] + tr_toks, inc=p.sem("s_dve"))
            t_dxb = p.emit("vector",
                           lambda v: v.tensor_tensor(
                               out=dxb[:, :].rearrange("p (h d) -> p h d", d=64),
                               in0=xst[:, :].rearrange("p (h d) -> p h d", d=64),
                               in1=pkt[:, 32:64].unsqueeze(2).broadcast_to(v128),
                               op=ALU.mult),
                           waits=[t_dxa, pv.get("dxb_war")], inc=p.sem("s_dve"))
            t_sp = p.emit("tensor",
                          lambda t, csl=csl: t.matmul(pD[:, 0:128], xsf(16, csl, 64),
                                                      xsf(17, csl, 64),
                                                      start=True, stop=True),
                          waits=[pv[("xsf", 16)], pv[("xsf", 17)], pv.get("pD_sfree")],
                          inc=p.sem("s_pe"))
            t_msb = p.emit("vector",
                           lambda v: v.tensor_tensor(out=msb[:], in0=pD[:, 0:128],
                                                     in1=consts["triu_bf"][:],
                                                     op=ALU.mult),
                           waits=[t_sp, pv.get("msb_war")], inc=p.sem("s_dve"))
            t_ym = None
            t_dp = t_am = None
            for r in range(2):
                t_ind = ctx.dma("sync", rhs64[32:64, :],
                                ctx.const_aps["indh"][:, r * 2048:(r + 1) * 2048],
                                waits=[pv.get("rhs_war")])
                v32 = (32, 16, 128)
                def indl(v, r=r, csl=csl):
                    return v.tensor_tensor(
                        out=rhs64[0:32, :].rearrange("p (h t) -> p h t", t=128),
                        in0=ind_sb[:, r * 2048:(r + 1) * 2048].rearrange(
                            "p (h t) -> p h t", t=128),
                        in1=Lc[:, csl].unsqueeze(1).broadcast_to(v32),
                        op=ALU.mult)
                t_diag = p.emit("vector", indl,
                                waits=[t_Lq, t_indc, pv.get("rhs_war")],
                                inc=p.sem("s_dve"))
                def mmD(t):
                    last = None
                    for n in range(4):
                        last = t.matmul(pD[:, n * 512:(n + 1) * 512], lhsT64[:],
                                        rhs64[:, n * 512:(n + 1) * 512],
                                        start=True, stop=True)
                    return last
                t_dp = p.emit("tensor", mmD,
                              waits=[t_ldma, t_ind, t_diag, t_ms, t_msb,
                                     pv.get("pD_dfree")],
                              inc=p.sem("s_pe"))
                def clamp(v):
                    v.tensor_scalar(out=gx[0][:], in0=pD[:, 0:1024], scalar1=0.0,
                                    scalar2=None, op0=ALU.min)
                    return v.tensor_scalar(out=gx[1][:], in0=pD[:, 1024:2048],
                                           scalar1=0.0, scalar2=None, op0=ALU.min)
                t_cl = p.emit("vector", clamp, waits=[t_dp, pv.get("gx_war")],
                              inc=p.sem("s_dve"))
                def gexp(s):
                    s.activation(out=gx[0][:], in_=gx[0][:], func=AF.Exp)
                    return s.activation(out=gx[1][:], in_=gx[1][:], func=AF.Exp)
                t_ex = p.emit("scalar", gexp, waits=[t_cl], inc=p.sem("s_act"))
                v8 = (128, 8, 128)
                def gmask(v):
                    v.tensor_tensor(
                        out=gx[0][:, :].rearrange("p (h t) -> p h t", t=128),
                        in0=gx[0][:, :].rearrange("p (h t) -> p h t", t=128),
                        in1=msb[:, :].unsqueeze(1).broadcast_to(v8), op=ALU.mult)
                    return v.tensor_tensor(
                        out=gx[1][:, :].rearrange("p (h t) -> p h t", t=128),
                        in0=gx[1][:, :].rearrange("p (h t) -> p h t", t=128),
                        in1=msb[:, :].unsqueeze(1).broadcast_to(v8), op=ALU.mult)
                t_gm = p.emit("vector", gmask, waits=[t_ex, t_msb], inc=p.sem("s_dve"))
                def mmA(t):
                    last = None
                    for n in range(4):
                        last = t.matmul(pD[0:64, n * 512:(n + 1) * 512], sel[:],
                                        rhs64[:, n * 512:(n + 1) * 512],
                                        start=True, stop=True)
                    return last
                t_am = p.emit("tensor", mmA, waits=[t_cl, t_ms, t_dp],
                              inc=p.sem("s_pe"))
                pv["rhs_war"] = t_am
                def aexp(s):
                    s.activation(out=cbp[0][0:64, :], in_=pD[0:64, 0:1024],
                                 func=AF.Exp)
                    return s.activation(out=cbp[1][0:64, :], in_=pD[0:64, 1024:2048],
                                        func=AF.Exp)
                t_ae = p.emit("scalar", aexp, waits=[t_am, pv.get("cb_war")],
                              inc=p.sem("s_act"))
                v864 = (64, 8, 128)
                def cmul(v, csl=csl):
                    v.tensor_tensor(
                        out=cbp[0][0:64, :].rearrange("p (h t) -> p h t", t=128),
                        in0=cbp[0][0:64, :].rearrange("p (h t) -> p h t", t=128),
                        in1=xsf(17, csl, 64).unsqueeze(1).broadcast_to(v864),
                        op=ALU.mult)
                    return v.tensor_tensor(
                        out=cbp[1][0:64, :].rearrange("p (h t) -> p h t", t=128),
                        in0=cbp[1][0:64, :].rearrange("p (h t) -> p h t", t=128),
                        in1=xsf(17, csl, 64).unsqueeze(1).broadcast_to(v864),
                        op=ALU.mult)
                t_cb = p.emit("vector", cmul, waits=[t_ae, pv[("xsf", 17)]],
                              inc=p.sem("s_dve"))
                def mmy(t, r=r):
                    last = None
                    for hh in range(16):
                        h = r * 16 + hh
                        pr = (h % 2) * 64
                        pos = (h // 2) * 128
                        gsl = slice((hh % 8) * 128, (hh % 8 + 1) * 128)
                        t.matmul(pY[pr:pr + 64, pos:pos + 128],
                                 xst[:, h * 64:(h + 1) * 64], gx[hh // 8][:, gsl],
                                 start=True, stop=False, skip_group_check=True)
                        last = t.matmul(pY[pr:pr + 64, pos:pos + 128],
                                        S_bf[:, h * 64:(h + 1) * 64],
                                        cbp[hh // 8][0:64, gsl],
                                        start=False, stop=True, skip_group_check=True)
                    return last
                t_ym = p.emit("tensor", mmy,
                              waits=[t_gm, t_cb, t_dxa, pv.get("S_bf_tok"),
                                     pv.get("pY_war")],
                              inc=p.sem("s_pe"))
                pv["gx_war"] = t_ym
                pv["cb_war"] = t_ym
                pv["pD_dfree"] = tokens_max([t_ae, t_cl])
            pv["msb_war"] = t_gm
            pv["lhs_war"] = t_dp
            pv["lneg_war"] = t_ldma
            pv["xst_war"] = None
            t_ac = p.emit("scalar",
                          lambda s, cend=cend: s.activation(
                              out=acol[:], in_=Lc[:, cend:cend + 1], func=AF.Exp),
                          waits=[t_Lq, pv.get("acol_war")], inc=p.sem("s_act"))
            t_ar = p.emit("tensor",
                          lambda t: t.matmul(pD[0:1, 0:32], acol[:],
                                             consts["ident_f"][0:32, 0:32],
                                             start=True, stop=True),
                          waits=[t_ac, t_ae], inc=p.sem("s_pe"))
            t_arc = p.emit("scalar",
                           lambda s: s.activation(out=arow_sb[:], in_=pD[0:1, 0:32],
                                                  func=AF.Copy),
                           waits=[t_ar], inc=p.sem("s_act"))
            t_asm = p.emit("tensor",
                           lambda t: t.matmul(pD[0:64, 64:96],
                                              consts["ones_f"][0:1, 0:64],
                                              arow_sb[0:1, :], start=True, stop=True),
                           waits=[t_arc], inc=p.sem("s_pe"))
            t_asmc = p.emit("scalar",
                            lambda s: s.activation(out=asm[:], in_=pD[0:64, 64:96],
                                                   func=AF.Copy),
                            waits=[t_asm], inc=p.sem("s_act"))
            pv["acol_war"] = t_ar
            def mmu(t):
                last = None
                for h in range(NH):
                    last = t.matmul(pD[0:64, h * 64:(h + 1) * 64], btk[:, 0:64],
                                    dxb[:, h * 64:(h + 1) * 64],
                                    start=True, stop=True, skip_group_check=True)
                return last
            t_um = p.emit("tensor", mmu, waits=[t_dxb, t_bt, t_asmc, t_ae],
                          inc=p.sem("s_pe"))
            pv["btk_war"] = t_um
            v_as = (64, 32, 64)
            t_s1 = p.emit("vector",
                          lambda v: v.tensor_tensor(
                              out=S_sb[:, :].rearrange("p (h d) -> p h d", d=64),
                              in0=S_sb[:, :].rearrange("p (h d) -> p h d", d=64),
                              in1=asm[:, :].unsqueeze(2).broadcast_to(v_as),
                              op=ALU.mult),
                          waits=[t_asmc, t_ym, t_ms], inc=p.sem("s_dve"))
            t_s2 = p.emit("vector",
                          lambda v: v.tensor_tensor(out=S_sb[:], in0=S_sb[:],
                                                    in1=pD[0:64, :], op=ALU.add),
                          waits=[t_s1, t_um], inc=p.sem("s_dve"))
            t_s3 = p.emit("scalar",
                          lambda s: s.activation(out=S_bf[:], in_=S_sb[:],
                                                 func=AF.Copy),
                          waits=[t_s2, t_ym], inc=p.sem("s_act"))
            pv["S_bf_tok"] = t_s3
            v16 = (128, 16, 128)
            ygt_v = dxb[:, :].rearrange("p (j t) -> p j t", t=128)
            xs_ch = xs_chunk(csl)
            t_e1 = p.emit("vector",
                          lambda v, xs_ch=xs_ch: v.tensor_tensor(
                              out=ygt_v, in0=xs_ch,
                              in1=d128[:, :].unsqueeze(2).broadcast_to(v16),
                              op=ALU.mult),
                          waits=[t_um, t_d128, pv.get("conv_done")],
                          inc=p.sem("s_dve"))
            t_e2 = p.emit("vector",
                          lambda v: v.tensor_tensor(
                              out=ygt_v, in0=ygt_v,
                              in1=pY[:, :].rearrange("p (j t) -> p j t", t=128),
                              op=ALU.add),
                          waits=[t_e1, t_ym], inc=p.sem("s_dve"))
            t_e3 = p.emit("vector",
                          lambda v, csl=csl: v.tensor_tensor(
                              out=ygt_v, in0=ygt_v, in1=zs_view[:, :, csl],
                              op=ALU.mult),
                          waits=[t_e2, pv.get(("pp", 0)), pv.get(("pp", 1))],
                          inc=p.sem("s_dve"))
            pv["pY_war"] = t_e2
            t_sq = p.emit("scalar",
                          lambda s: s.activation(out=xst[:], in_=dxb[:],
                                                 func=AF.Square),
                          waits=[t_e3, t_ym], inc=p.sem("s_act"))
            def mst(t):
                last = None
                for j in range(16):
                    last = t.matmul(pD[0:1, 0:128], om2k[:],
                                    xst[:, j * 128:(j + 1) * 128],
                                    start=(j == 0), stop=(j == 15))
                return last
            t_st = p.emit("tensor", mst, waits=[t_sq, t_s2, t_um], inc=p.sem("s_pe"))
            t_r1 = p.emit("scalar",
                          lambda s: s.activation(out=rowA, in_=pD[0:1, 0:128],
                                                 func=AF.Ln, bias=eps1[:]),
                          waits=[t_st, eps1_tok, pv.get("rowA_war")],
                          inc=p.sem("s_act"))
            t_r2 = p.emit("scalar",
                          lambda s: s.activation(out=rowA, in_=rowA, func=AF.Exp,
                                                 scale=-0.5),
                          waits=[t_r1], inc=p.sem("s_act"))
            t_pb = p.emit("tensor",
                          lambda t: t.matmul(pD[:, 128:256], consts["ones_f"][0:1, :],
                                             rowA, start=True, stop=True),
                          waits=[t_r2, t_st], inc=p.sem("s_pe"))
            t_ap1 = p.emit("vector",
                           lambda v, csl=csl: v.tensor_tensor(
                               out=yr_view[:, :, csl], in0=ygt_v,
                               in1=pD[:, 128:256].unsqueeze(1).broadcast_to(v16),
                               op=ALU.mult),
                           waits=[t_pb, t_e3, pv.get("yrm_war")],
                           inc=p.sem("s_dve"))
            t_ap2 = p.emit("vector",
                           lambda v, csl=csl: v.tensor_tensor(
                               out=yr_view[:, :, csl], in0=yr_view[:, :, csl],
                               in1=ssm16[:, :].unsqueeze(2).broadcast_to(v16),
                               op=ALU.mult),
                           waits=[t_ap1, sv_tok], inc=p.sem("s_dve"))
            pv[("yq", q)] = t_ap2
            pv["dxb_war"] = t_ap1
            pv["xst_war"] = t_st
            pv["rowA_war"] = t_pb
            pv["pD_sfree"] = tokens_max([t_r1, t_ap1, t_s2])
            pv["pD_dfree"] = tokens_max([t_r1, t_ap1, t_s2, t_msb])
            pv["rtmp_war"] = t_rex
            pv["rstg_war"] = t_rdma
            pv["pkt_war"] = t_dxb
            pv["Lc_last_pe"] = t_ar
        y_ready = tokens_max([pv[("yq", q)] for q in range(NCHB)])
        op_mm_last = None
        for k in range(16):
            wb = k % 2
            ws = wb * 1024
            ld = ctx.dma("sync", wbuf[:, ws:ws + 1024],
                         opT.ap()[k * 128:(k + 1) * 128, :],
                         waits=[w_tok, pv.get(("wz", wb))])
            def mmo(t, ws=ws, k=k):
                last = None
                for j2 in range(8):
                    reg = pD[:, j2 * 512:(j2 + 1) * 512] if j2 < 4 else \
                          pY[:, (j2 - 4) * 512:(j2 - 3) * 512]
                    last = t.matmul(reg, wbuf[:, ws + j2 * 128:ws + (j2 + 1) * 128],
                                    zsil[:, k * 512:(k + 1) * 512],
                                    start=(k == 0), stop=(k == 15))
                return last
            w = [ld, y_ready]
            if k == 0:
                w += [pv.get("pD_sfree"), pv.get("pD_dfree"), pv.get("pY_war")]
            op_mm_last = p.emit("tensor", mmo, waits=w, inc=p.sem("s_pe"))
            pv[("wz", wb)] = op_mm_last
        op_evs = []
        for j2 in range(8):
            reg = pD[:, j2 * 512:(j2 + 1) * 512] if j2 < 4 else \
                  pY[:, (j2 - 4) * 512:(j2 - 3) * 512]
            def evo(v, j2=j2, reg=reg, bsl=bsl):
                return v.tensor_tensor(out=xn[j2][:, bsl], in0=xn[j2][:, bsl],
                                       in1=reg, op=ALU.add)
            op_evs.append(p.emit("vector", evo, waits=[op_mm_last],
                                 inc=p.sem("s_dve")))
        t_op = tokens_max(op_evs)
        pv["pD_free"] = t_op
        pv["pD_sfree"] = t_op
        pv["pD_dfree"] = t_op
        pv["pY_war"] = t_op
        pv["yrm_war"] = op_mm_last
        pv[("xi", b, 0)] = t_op
        pv[("xi", b, 1)] = t_op
        pv["zsil_war"] = op_mm_last
        pv["ext_war"] = tokens_max([pv.get(("yq", NCHB - 1)), pv.get("xst_war")])
        pv["xsf_war"] = tokens_max([pv.get(("yq", NCHB - 1)), pv.get("xst_war")])
        pv["sp1_war"] = pv.get("pk_war")
        pv["dt1_war"] = pv.get("pk_war")
        pv["dtrow_war"] = t_Lq
        pv["Lc_war"] = tokens_max([pv.get("lneg_war"), pv.get("rtmp_war"),
                                   pv.get("Lc_last_pe")])
    xi_tok = tokens_max([pv[("xi", b, hq)] for b in range(NB)
                         for hq in range(NCHB // 2)])

    done = []

    def m_tile(k):
        if k < 16:
            return zsil[:, k * 512:(k + 1) * 512]
        if k < 20:
            return xst[:, (k - 16) * 512:(k - 15) * 512]
        if k < 24:
            return dxb[:, (k - 20) * 512:(k - 19) * 512]
        return arena[:, 4096 + (k - 24) * 512:4096 + (k - 23) * 512]

    def xn2_t(j):
        return arena[:, j * 512:(j + 1) * 512]

    pv2 = {"pfree": tokens_max([xi_tok, pv.get("xsf_war"), pv.get("conv_done")]),
           "wbuf_war": tokens_max([pv.get(("wz", 0)), pv.get(("wz", 1))])}
    for k in range(32):
        pv2[("m", k)] = pv2["pfree"]
    sgs = ctx.pool("ln_xb1", [128, T // 2], BF16)[:, 0:512]
    for tb in range(NB):
        tsl = slice(tb * TB, (tb + 1) * TB)
        sq_toks = []
        for j in range(ND):
            def sqf(s, j=j, tsl=tsl, tb=tb):
                return s.activation(out=sgs, in_=xn[j][:, tsl], func=AF.Square)
            t_sq = p.emit("scalar", sqf,
                          waits=[xi_tok, pv2.get("sgs_war"), sq_toks[-1] if sq_toks else None],
                          inc=p.sem("s_act"))
            def msq(t, j=j):
                return t.matmul(pD[0:1, 0:512], consts["ones_mean"][:], sgs,
                                start=(j == 0), stop=(j == ND - 1))
            t_mq = p.emit("tensor", msq, waits=[t_sq, pv2.get("pfree")],
                          inc=p.sem("s_pe"))
            sq_toks.append(t_mq)
        rowB = sc_fA[0:1, 0:512]
        t_r1 = p.emit("scalar",
                      lambda s: s.activation(out=rowB, in_=pD[0:1, 0:512], func=AF.Ln,
                                             bias=eps1[:]),
                      waits=[sq_toks[-1], pv2.get("rowB_war")], inc=p.sem("s_act"))
        t_r2 = p.emit("scalar",
                      lambda s: s.activation(out=rowB, in_=rowB, func=AF.Exp,
                                             scale=-0.5),
                      waits=[t_r1], inc=p.sem("s_act"))
        def pbm(t):
            return t.matmul(pD[:, 512:1024], consts["ones_f"][0:1, :], rowB,
                            start=True, stop=True)
        t_pb = p.emit("tensor", pbm, waits=[t_r2], inc=p.sem("s_pe"))
        ap_toks = []
        for j in range(ND):
            def ap1(v, j=j, tsl=tsl):
                return v.tensor_tensor(out=xn2_t(j),
                                       in0=xn[j][:, tsl], in1=pD[:, 512:1024],
                                       op=ALU.mult)
            t_a1 = p.emit("vector", ap1, waits=[t_pb, pv2.get("xn2_war")],
                          inc=p.sem("s_dve"))
            def ap2(v, j=j):
                return v.tensor_scalar(out=xn2_t(j), in0=xn2_t(j),
                                       scalar1=rw8[:, j:j + 1], scalar2=None,
                                       op0=ALU.mult)
            ap_toks.append(p.emit("vector", ap2, waits=[t_a1, sv_tok],
                                  inc=p.sem("s_dve")))
        xn2_tok = tokens_max(ap_toks)
        m_toks = []
        prev_h1 = {"wa": pv2.get("wbuf_war"), "wg": pv2.get("wbuf_war")}
        for jg in range(8):
            lda = ctx.dma("sync",
                          wbuf[:, 0:4096].rearrange("p (i c) -> p i c", c=512),
                          fc1T.ap()[:, jg * 512:(jg + 1) * 512].rearrange(
                              "(i p) c -> p i c", p=128),
                          waits=[w_tok, prev_h1.get("wa")])
            ma_toks = []
            for pr in range(4):
                pa = [pD[:, 1024:1536], pD[:, 1536:2048], pY[:, 0:512],
                      pY[:, 512:1024]][pr]
                def mh(t, pr=pr, pa=pa):
                    last = None
                    for i in range(ND):
                        last = t.matmul(pa, wbuf[:, i * 512 + pr * 128:i * 512 + (pr + 1) * 128],
                                        xn2_t(i),
                                        start=(i == 0), stop=(i == ND - 1))
                    return last
                ma_toks.append(p.emit("tensor", mh,
                              waits=[lda, xn2_tok, prev_h1.get(("pa", pr)),
                                     (pv2.get("pfree") if pr >= 2 else None),
                                     t_pb],
                              inc=p.sem("s_pe")))
            ldg = ctx.dma("sync",
                          wbuf[:, 0:4096].rearrange("p (i c) -> p i c", c=512),
                          fc1T.ap()[:, 4096 + jg * 512:4096 + (jg + 1) * 512].rearrange(
                              "(i p) c -> p i c", p=128),
                          waits=[w_tok] + ma_toks)
            last_pair_mm = None
            for pr in range(4):
                k = jg * 4 + pr
                pa = [pD[:, 1024:1536], pD[:, 1536:2048], pY[:, 0:512],
                      pY[:, 512:1024]][pr]
                t_ma = ma_toks[pr]
                pgr = [pY[:, 1024:1536], pY[:, 1536:2048], pD[:, 0:512],
                       pD[:, 512:1024]][pr]
                def mg2(t, pr=pr, pgr=pgr):
                    last = None
                    for i in range(ND):
                        last = t.matmul(pgr,
                                        wbuf[:, i * 512 + pr * 128:i * 512 + (pr + 1) * 128],
                                        xn2_t(i),
                                        start=(i == 0), stop=(i == ND - 1))
                    return last
                wg_extra = None
                if pr >= 2:
                    wg_extra = tokens_max([t_r1, ap_toks[-1] if ap_toks else None])
                t_mg = p.emit("tensor", mg2,
                              waits=[ldg, xn2_tok, prev_h1.get(("pg", pr)), wg_extra],
                              inc=p.sem("s_pe"))
                last_pair_mm = t_mg
                t_sg = p.emit("scalar",
                              lambda s, pgr=pgr: s.activation(out=sgs, in_=pgr,
                                                              func=AF.Silu),
                              waits=[t_mg, prev_h1.get("sgs")], inc=p.sem("s_act"))
                def mev(v, k=k, pa=pa):
                    return v.tensor_tensor(out=m_tile(k), in0=pa, in1=sgs,
                                           op=ALU.mult)
                t_me = p.emit("vector", mev, waits=[t_sg, t_ma, pv2.get(("m", k))],
                              inc=p.sem("s_dve"))
                m_toks.append(t_me)
                prev_h1[("pa", pr)] = t_me
                prev_h1[("pg", pr)] = t_me
                prev_h1["sgs"] = t_me
            prev_h1["wa"] = last_pair_mm
            prev_h1["wg"] = last_pair_mm
        pv2["sgs_war"] = prev_h1["sgs"]
        m_tok = tokens_max(m_toks[-4:])
        fc_last = None
        for k in range(32):
            wb = k % 2
            ws = wb * 1024
            ld = ctx.dma("scalar", wbuf[:, ws:ws + 1024],
                         fc2T.ap()[k * 128:(k + 1) * 128, :],
                         waits=[w_tok, pv2.get(("wf", wb)), (m_tok if k < 2 else None)])
            def mf(t, ws=ws, k=k):
                last = None
                for j2 in range(8):
                    reg = pD[:, j2 * 512:(j2 + 1) * 512] if j2 < 4 else \
                          pY[:, (j2 - 4) * 512:(j2 - 3) * 512]
                    last = t.matmul(reg, wbuf[:, ws + j2 * 128:ws + (j2 + 1) * 128],
                                    m_tile(k), start=(k == 0), stop=(k == 31))
                return last
            w = [ld, m_toks[k], pv2.get(("wf", wb))]
            if k == 0:
                w.append(tokens_max(m_toks))
            fc_last = p.emit("tensor", mf, waits=w, inc=p.sem("s_pe"))
            pv2[("wf", wb)] = fc_last
        evs = []
        for j2 in range(8):
            reg = pD[:, j2 * 512:(j2 + 1) * 512] if j2 < 4 else \
                  pY[:, (j2 - 4) * 512:(j2 - 3) * 512]
            def ev1(v, j2=j2, reg=reg, tsl=tsl):
                return v.tensor_tensor(out=x[j2][:, tsl], in0=x[j2][:, tsl],
                                       in1=reg, op=ALU.add)
            t_v1 = p.emit("vector", ev1, waits=[fc_last], inc=p.sem("s_dve"))
            def ev2(v, j2=j2, tsl=tsl):
                return v.tensor_tensor(out=x[j2][:, tsl], in0=x[j2][:, tsl],
                                       in1=xn[j2][:, tsl], op=ALU.add)
            t_v2 = p.emit("vector", ev2, waits=[t_v1], inc=p.sem("s_dve"))
            evs.append(t_v2)
        pv2["pfree"] = tokens_max(evs)
        pv2["rowB_war"] = t_pb
        pv2["xn2_war"] = tokens_max(m_toks[-4:] + [last_pair_mm])
        pv2["wbuf_war"] = fc_last
        for k in range(32):
            pv2[("m", k)] = fc_last
        done = evs
    cmY.__exit__(None, None, None)
    cmD.__exit__(None, None, None)
    return tokens_max(done)



def emit_moe_prep(ctx, ins):
    wgu = ctx.dram([32 * 1024, 1024], BF16, "moe_wgu")
    t1 = ctx.dma("gpsimd", wgu.ap(), ins["w_gate_up"].rearrange("e h c -> (e h) c"))
    wdn = ctx.dram([32 * 512, 1024], BF16, "moe_wdn")
    t2 = ctx.dma("gpsimd", wdn.ap(), ins["w_down"].rearrange("e h c -> (e h) c"))
    wsg, t3 = emit_prep_bf16(ctx, ins["shared_w_gate_up"][:, 0:512], 1024, 512)
    wsu, t4 = emit_prep_bf16(ctx, ins["shared_w_gate_up"][:, 512:1024], 1024, 512)
    wsd, t5 = emit_prep_bf16(ctx, ins["shared_w_down"], 512, 1024)
    return (dict(wgu=wgu, wdn=wdn, wsg=wsg, wsu=wsu, wsd=wsd),
            tokens_max([t1, t2, t3, t4, t5]))


def emit_moe_route(ctx, consts, xn, xn_tok, W, w_tok, counts_out_ap, gf_out_ap,
                   sel_out_ap, tag="mr"):
    import os
    MRS = int(os.environ.get("MRS", 9))
    MRC = int(os.environ.get("MRC", 16))
    p = ctx.p
    E, K = 32, 4
    gwT = ctx.pool("moe_gwT", [128, 256], BF16)
    gwf = ctx.pool("moe_gwf", [32, 1024], F32)
    self_ = ctx.pool("moe_sel", [32, T], BF16)
    gf = ctx.pool("moe_gf", [32, T], BF16)
    ltw = ctx.pool("moe_lt", [128, 96], F32)
    cols = ctx.pool("moe_cols", [128, 8], F32)
    selc = ctx.pool("moe_selc", [128, 64], F32)
    gtf = ctx.pool("moe_gtf", [128, 32], F32)
    pay = ctx.pool("moe_pay", [16, 128], F32)
    carry = ctx.pool("moe_cy", [32, 2], F32)

    pMA, cmA = ctx.psum_scoped([128, 2048], F32, tag + "_pa")
    pTb, cmTb = ctx.psum_scoped([128, 512], F32, tag + "_pt")
    pv = {}

    t_gw = ctx.dma("sync", gwf[:], W["gw_f32"][:, :],
                   waits=[ctx._moe_gwf_war if hasattr(ctx, "_moe_gwf_war") else None])
    def trg(t):
        last = None
        for i in range(8):
            last = t.transpose(pTb[:, i * 64:i * 64 + 32],
                               gwf[0:32, i * 128:(i + 1) * 128],
                               consts["ident_f"][0:32, 0:32])
        return last
    t_tr = p.emit("tensor", trg, waits=[t_gw, consts["ready"], xn_tok],
                  inc=p.sem("s_pe"))
    t_gwT = p.emit("vector",
                   lambda v: v.tensor_copy(
                       out=gwT[:, :].rearrange("p (i c) -> p i c", c=32),
                       in_=pTb[:, :].rearrange("p (i c) -> p i c", c=64)[:, :, 0:32]),
                   waits=[t_tr], inc=p.sem("s_dve"))
    lt0 = ltw[:, 0:32]
    lt1 = ltw[:, 32:64]
    lsc = ltw[:, 64:96]
    mxc = cols[:, 0:1]
    denc = cols[:, 1:2]
    recc = cols[:, 2:3]
    for cc in range(MRC):
        csl = slice(cc * 128, (cc + 1) * 128)
        psl = slice((cc % 2) * 512, (cc % 2) * 512 + 32)
        def mml(t, csl=csl, psl=psl):
            last = None
            for i in range(ND):
                last = t.matmul(pMA[:, psl], xn[i][:, csl],
                                gwT[:, i * 32:(i + 1) * 32],
                                start=(i == 0), stop=(i == ND - 1))
            return last
        t_t1 = p.emit("tensor", mml,
                      waits=[t_gwT, xn_tok, pv.get(("ltr", cc % 2))],
                      inc=p.sem("s_pe"))
        if MRS < 1:
            pv[("ltr", cc % 2)] = t_t1
            pv[("selchunk", cc)] = t_t1
            pv[("gfchunk", cc)] = t_t1
            continue
        t_l0 = p.emit("vector",
                      lambda v, psl=psl: v.tensor_copy(out=lt0, in_=pMA[:, psl]),
                      waits=[t_t1, pv.get("lt0_war")], inc=p.sem("s_dve"))
        t_l1 = p.emit("vector",
                      lambda v: v.tensor_copy(out=lt1, in_=lt0),
                      waits=[t_l0, pv.get("lt1_war")], inc=p.sem("s_dve"))
        pv[("ltr", cc % 2)] = t_l0
        if MRS < 2:
            pv[("selchunk", cc)] = t_l1
            pv[("gfchunk", cc)] = t_l1
            continue
        sel_tok = None
        for rnd in range(K):
            t_mx = p.emit("vector",
                          lambda v: v.tensor_reduce(out=mxc, in_=lt1, axis=AX.X,
                                                    op=ALU.max),
                          waits=[t_l1, pv.get("mx_war")], inc=p.sem("s_dve"))
            t_im = p.emit("vector",
                          lambda v: v.tensor_scalar(out=lsc, in0=lt1, scalar1=mxc,
                                                    scalar2=None, op0=ALU.is_ge),
                          waits=[t_mx, pv.get("lsc_war")], inc=p.sem("s_dve"))
            if rnd == 0:
                t_se = p.emit("vector",
                              lambda v: v.tensor_copy(out=gtf[:], in_=lsc),
                              waits=[t_im, pv.get("gtf_war")], inc=p.sem("s_dve"))
            else:
                t_se = p.emit("vector",
                              lambda v: v.tensor_tensor(out=gtf[:], in0=gtf[:],
                                                        in1=lsc, op=ALU.add),
                              waits=[t_im], inc=p.sem("s_dve"))
            t_bg = p.emit("vector",
                          lambda v: v.tensor_scalar(out=lsc, in0=lsc, scalar1=1e30,
                                                    scalar2=None, op0=ALU.mult),
                          waits=[t_se], inc=p.sem("s_dve"))
            t_l1 = p.emit("vector",
                          lambda v: v.tensor_tensor(out=lt1, in0=lt1, in1=lsc,
                                                    op=ALU.subtract),
                          waits=[t_bg], inc=p.sem("s_dve"))
            sel_tok = t_se
            pv["mx_war"] = t_im
            pv["lsc_war"] = t_l1
        if MRS < 3:
            pv[("selchunk", cc)] = sel_tok
            pv[("gfchunk", cc)] = sel_tok
            continue
        t_eg = p.emit("scalar",
                      lambda s: s.activation(out=lt1, in_=lt0, func=AF.Exp),
                      waits=[t_l1, sel_tok], inc=p.sem("s_act"))
        t_es = p.emit("vector",
                      lambda v: v.tensor_tensor(out=lt1, in0=lt1, in1=gtf[:],
                                                op=ALU.mult),
                      waits=[t_eg, sel_tok], inc=p.sem("s_dve"))
        t_dn = p.emit("vector",
                      lambda v: v.tensor_reduce(out=denc, in_=lt1, axis=AX.X,
                                                op=ALU.add),
                      waits=[t_es, pv.get("den_war")], inc=p.sem("s_dve"))
        t_rc = p.emit("vector",
                      lambda v: v.reciprocal(out=recc, in_=denc),
                      waits=[t_dn, pv.get("rec_war")], inc=p.sem("s_dve"))
        t_gt = p.emit("vector",
                      lambda v: v.tensor_scalar(out=selc[:, 32:64], in0=lt1,
                                                scalar1=recc, scalar2=None,
                                                op0=ALU.mult),
                      waits=[t_rc, pv.get("gtb_war")], inc=p.sem("s_dve"))
        t_sb = p.emit("vector",
                      lambda v: v.tensor_copy(out=selc[:, 0:32], in_=gtf[:]),
                      waits=[sel_tok, pv.get("selb_war")], inc=p.sem("s_dve"))
        pv["lt0_war"] = t_eg
        pv["lt1_war"] = t_gt
        pv["gtf_war"] = t_sb
        pv["den_war"] = t_rc
        if MRS < 4:
            pv[("selchunk", cc)] = t_sb
            pv[("gfchunk", cc)] = t_gt
            continue
        t_t2 = p.emit("tensor",
                      lambda t, cc=cc: t.matmul(pMA[0:32, 1024 + (cc % 2) * 512:
                                                    1152 + (cc % 2) * 512],
                                                selc[:, 0:32],
                                                consts["ident_f"][:],
                                                start=True, stop=True),
                      waits=[t_sb, t_gwT, pv.get(("str", cc % 2))], inc=p.sem("s_pe"))
        t_sv = p.emit("vector",
                      lambda v, csl=csl, cc=cc: v.tensor_copy(
                          out=self_[:, csl],
                          in_=pMA[0:32, 1024 + (cc % 2) * 512:1152 + (cc % 2) * 512]),
                      waits=[t_t2], inc=p.sem("s_dve"))
        t_t3 = p.emit("tensor",
                      lambda t, cc=cc: t.matmul(pMA[0:32, 1024 + (cc % 2) * 512:
                                                    1152 + (cc % 2) * 512],
                                                selc[:, 32:64],
                                                consts["ident_f"][:],
                                                start=True, stop=True),
                      waits=[t_gt, t_sv], inc=p.sem("s_pe"))
        t_gv = p.emit("vector",
                      lambda v, csl=csl, cc=cc: v.tensor_copy(
                          out=gf[:, csl],
                          in_=pMA[0:32, 1024 + (cc % 2) * 512:1152 + (cc % 2) * 512]),
                      waits=[t_t3], inc=p.sem("s_dve"))
        pv[("str", cc % 2)] = t_gv
        pv["rec_war"] = t_gt
        pv["gtb_war"] = t_t3
        pv["selb_war"] = t_t2
        pv[("selchunk", cc)] = t_sv
        pv[("gfchunk", cc)] = t_gv
    sel_done = tokens_max([pv[("selchunk", cc)] for cc in range(MRC)])
    gf_done = tokens_max([pv[("gfchunk", cc)] for cc in range(MRC)])
    if MRS < 5:
        cmTb.__exit__(None, None, None)
        cmA.__exit__(None, None, None)
        return tokens_max([sel_done, gf_done])
    t_cn = p.emit("vector",
                  lambda v: v.tensor_reduce(out=carry[:, 0:1], in_=self_[:, :],
                                            axis=AX.X, op=ALU.add),
                  waits=[sel_done], inc=p.sem("s_dve"))
    t_cr = p.emit("tensor",
                  lambda t: t.matmul(pMA[0:1, 1984:2016], carry[:, 0:1],
                                     consts["ident_f"][0:32, 0:32],
                                     start=True, stop=True),
                  waits=[t_cn, gf_done], inc=p.sem("s_pe"))
    t_pz = p.emit("vector", lambda v: v.memset(pay[:], 0.0), inc=p.sem("s_dve"))
    t_cw = p.emit("scalar",
                  lambda s: s.activation(out=pay[0:1, 0:32], in_=pMA[0:1, 1984:2016],
                                         func=AF.Copy),
                  waits=[t_cr, t_pz], inc=p.sem("s_act"))
    t_co = ctx.dma("sync", counts_out_ap[:, :], pay[0:1, 0:32], waits=[t_cw])
    t_go = ctx.dma("sync", gf_out_ap[:, :], gf[:, :], waits=[gf_done])
    t_so = ctx.dma("sync", sel_out_ap[:, :], self_[:, :], waits=[sel_done])
    cmTb.__exit__(None, None, None)
    cmA.__exit__(None, None, None)
    return tokens_max([t_co, t_go, t_so])


def emit_moe_experts(ctx, consts, x, xn, xn_tok, W, w_tok, gf_in_ap, sel_in_ap,
                     counts_all_ap, pmask8_ap, tag="me"):
    p = ctx.p
    E = 32
    sh_tok = emit_glu_ffn(ctx, consts, x, xn, xn_tok, W["wsg"], W["wsu"], W["wsd"],
                          512, w_tok, tc_width=512, slab_hc=4)
    self_ = ctx.pool("moe_sel", [32, T], BF16)
    gf = ctx.pool("moe_gf", [32, T], BF16)
    ca8 = ctx.pool("moe_ca8", [8, 32], F32)
    pm8 = ctx.pool("moe_pm8", [8, 2], F32)
    offc = ctx.pool("moe_off", [32, 2], F32)
    carry = ctx.pool("moe_cy", [32, 2], F32)
    gb_sb = ctx.pool("moe_gb", [128, T], BF16)
    hb = ctx.pool("moe_hb", [128, 8192], BF16)
    sgm = ctx.pool("moe_sg1", [128, 512], BF16)
    grow = ctx.pool("moe_grow", [1, 512], BF16)
    wbuf = ctx.pool("ffn_wbuf", [128, ND * 1024], BF16)
    sc_fA = ctx.pool("sc_fA", [128, T // 2], F32)
    rloc = sc_fA[0:32, 0:128]
    gfd = ctx.dram([32, T], BF16, tag + "_gfd")
    pMA, cmA = ctx.psum_scoped([128, 2048], F32, tag + "_pa")
    pMB, cmB = ctx.psum_scoped([128, 1536], F32, tag + "_pb")
    pv = {}
    war = getattr(ctx, "_moe_exp_war", None)
    t_gf = ctx.dma("sync", gf[:, :], gf_in_ap[:, :], waits=[war])
    t_sl = ctx.dma("sync", self_[:, :], sel_in_ap[:, :], waits=[war])
    t_ca = ctx.dma("sync", ca8[:], counts_all_ap[:, :])
    t_p8 = ctx.dma("sync", pm8[:], pmask8_ap[:, :])
    gf_done = t_gf
    sel_done = t_sl
    def mmoff(t):
        return t.matmul(pMA[0:32, 1024:1026], ca8[:], pm8[:],
                        start=True, stop=True)
    t_of = p.emit("tensor", mmoff, waits=[t_ca, t_p8, xn_tok, sh_tok],
                  inc=p.sem("s_pe"))
    t_oe = p.emit("scalar",
                  lambda s: s.activation(out=offc[:, 0:1], in_=pMA[0:32, 1024:1025],
                                         func=AF.Copy),
                  waits=[t_of], inc=p.sem("s_act"))
    t_th = p.emit("vector",
                  lambda v: v.tensor_scalar(out=offc[:, 1:2], in0=offc[:, 0:1],
                                            scalar1=-1.0, scalar2=1023.5,
                                            op0=ALU.mult, op1=ALU.add),
                  waits=[t_oe], inc=p.sem("s_dve"))
    t_cz = p.emit("vector", lambda v: v.memset(carry[:, 1:2], 0.0),
                  inc=p.sem("s_dve"))
    cy_tok = t_cz
    for cc in range(16):
        csl = slice(cc * 128, (cc + 1) * 128)
        t_s1 = p.emit("vector",
                      lambda v, csl=csl: v.tensor_tensor_scan(
                          out=rloc, data0=consts["ones_f"][0:32, 0:128],
                          data1=self_[:, csl], initial=0.0,
                          op0=ALU.mult, op1=ALU.add),
                      waits=[sel_done, cy_tok, pv.get("rloc_war")],
                      inc=p.sem("s_dve"))
        t_s2 = p.emit("vector",
                      lambda v, csl=csl: v.tensor_tensor(out=rloc, in0=rloc,
                                                         in1=self_[:, csl],
                                                         op=ALU.subtract),
                      waits=[t_s1], inc=p.sem("s_dve"))
        t_s3 = p.emit("vector",
                      lambda v: v.tensor_scalar(out=rloc, in0=rloc,
                                                scalar1=carry[:, 1:2], scalar2=None,
                                                op0=ALU.add),
                      waits=[t_s2], inc=p.sem("s_dve"))
        t_c1 = p.emit("vector",
                      lambda v, csl=csl: v.tensor_reduce(out=carry[:, 0:1],
                                                         in_=self_[:, csl],
                                                         axis=AX.X, op=ALU.add),
                      waits=[t_s1], inc=p.sem("s_dve"))
        cy_tok = p.emit("vector",
                        lambda v: v.tensor_tensor(out=carry[:, 1:2],
                                                  in0=carry[:, 1:2],
                                                  in1=carry[:, 0:1], op=ALU.add),
                        waits=[t_c1, t_s3], inc=p.sem("s_dve"))
        t_kp = p.emit("vector",
                      lambda v: v.tensor_scalar(out=rloc, in0=rloc,
                                                scalar1=offc[:, 1:2], scalar2=None,
                                                op0=ALU.is_le),
                      waits=[t_s3, t_th, cy_tok], inc=p.sem("s_dve"))
        t_gm = p.emit("vector",
                      lambda v, csl=csl: v.tensor_tensor(out=gf[:, csl],
                                                         in0=gf[:, csl], in1=rloc,
                                                         op=ALU.mult),
                      waits=[t_kp, gf_done], inc=p.sem("s_dve"))
        pv["rloc_war"] = t_gm
        pv[("gfin", cc)] = t_gm
    gfin_done = tokens_max([pv[("gfin", cc)] for cc in range(16)])
    t_gfd = ctx.dma("sync", gfd.ap(), gf[:, :], waits=[gfin_done])

    x_toks = []
    wgu, wdn = W["wgu"], W["wdn"]
    pv[("wgu", 0)] = sh_tok
    pv[("wgu", 1)] = sh_tok
    pv["wdn_war"] = sh_tok
    for e in range(E):
        gbe = []
        for tc in range(4):
            tsl = slice(tc * 512, (tc + 1) * 512)
            t_gr = ctx.dma("sync", grow[:], gfd.ap()[e:e + 1, tsl],
                           waits=[t_gfd, pv.get("grow_war")])
            def mgb(t, tc=tc):
                return t.matmul(pMA[:, 512 + (tc % 2) * 512:512 + (tc % 2) * 512 + 512],
                                consts["ones_row"][0:1, 0:128], grow[:],
                                start=True, stop=True)
            t_gb = p.emit("tensor", mgb,
                          waits=[t_gr, pv.get(("gbp", tc % 2)), pv.get("pA_war")],
                          inc=p.sem("s_pe"))
            pv["grow_war"] = t_gb
            t_gc = p.emit("scalar",
                          lambda s, tc=tc: s.activation(
                              out=gb_sb[:, tc * 512:(tc + 1) * 512],
                              in_=pMA[:, 512 + (tc % 2) * 512:512 + (tc % 2) * 512 + 512],
                              func=AF.Copy),
                          waits=[t_gb, pv.get("gbsb_war")], inc=p.sem("s_act"))
            pv[("gbp", tc % 2)] = t_gc
            gbe.append(t_gc)
        gbe_tok = tokens_max(gbe)
        lddn = []
        for k2 in range(4):
            lddn.append(ctx.dma("scalar",
                                wbuf[:, 4096 + k2 * 1024:4096 + (k2 + 1) * 1024],
                                wdn.ap()[e * 512 + k2 * 128:e * 512 + (k2 + 1) * 128, :],
                                waits=[w_tok, pv.get("wdn_war")]))
        for j in range(4):
            jb = j % 2
            ldg = ctx.dma("sync",
                          wbuf[:, jb * 2048:jb * 2048 + 1024].rearrange(
                              "p (i c) -> p i c", c=128),
                          wgu.ap()[e * 1024:(e + 1) * 1024,
                                   j * 128:(j + 1) * 128].rearrange(
                              "(i p) c -> p i c", p=128),
                          waits=[w_tok, pv.get(("wgu", jb))])
            ldu = ctx.dma("sync",
                          wbuf[:, jb * 2048 + 1024:jb * 2048 + 2048].rearrange(
                              "p (i c) -> p i c", c=128),
                          wgu.ap()[e * 1024:(e + 1) * 1024,
                                   512 + j * 128:512 + (j + 1) * 128].rearrange(
                              "(i p) c -> p i c", p=128),
                          waits=[w_tok, pv.get(("wgu", jb))])
            last_mm = None
            for tc in range(4):
                tsl = slice(tc * 512, (tc + 1) * 512)
                tb = tc % 2
                pg = pMA[:, 0:512] if tb == 0 else pMA[:, 1536:2048]
                pu = pMB[:, tb * 512:tb * 512 + 512]
                def mgu(t, jb=jb, tsl=tsl, pg=pg, pu=pu):
                    last = None
                    for i in range(ND):
                        t.matmul(pg, wbuf[:, jb * 2048 + i * 128:jb * 2048 + (i + 1) * 128],
                                 xn[i][:, tsl], start=(i == 0), stop=(i == ND - 1))
                        last = t.matmul(pu, wbuf[:, jb * 2048 + 1024 + i * 128:
                                                 jb * 2048 + 1024 + (i + 1) * 128],
                                        xn[i][:, tsl], start=(i == 0), stop=(i == ND - 1))
                    return last
                t_mm = p.emit("tensor", mgu,
                              waits=[ldg, ldu, xn_tok, pv.get(("pgu", tb)),
                                     (t_oe if e == 0 else None)],
                              inc=p.sem("s_pe"))
                last_mm = t_mm
                t_si = p.emit("scalar",
                              lambda s, pg=pg: s.activation(out=sgm[:], in_=pg,
                                                            func=AF.Silu),
                              waits=[t_mm, pv.get("sgm_war")], inc=p.sem("s_act"))
                hsl = slice((j * 4 + tc) * 512, (j * 4 + tc + 1) * 512)
                t_h1 = p.emit("vector",
                              lambda v, pu=pu, hsl=hsl: v.tensor_tensor(
                                  out=hb[:, hsl], in0=pu, in1=sgm[:], op=ALU.mult),
                              waits=[t_si, pv.get(("hbs", j * 4 + tc))],
                              inc=p.sem("s_dve"))
                t_h2 = p.emit("vector",
                              lambda v, hsl=hsl, tc=tc: v.tensor_tensor(
                                  out=hb[:, hsl], in0=hb[:, hsl],
                                  in1=gb_sb[:, tc * 512:(tc + 1) * 512], op=ALU.mult),
                              waits=[t_h1, gbe[tc]], inc=p.sem("s_dve"))
                pv[("pgu", tb)] = t_h1
                pv["sgm_war"] = t_h1
                pv[("hbready", j * 4 + tc)] = t_h2
            pv[("wgu", jb)] = last_mm
        pv["gbsb_war"] = tokens_max([pv[("hbready", s_)] for s_ in range(16)])
        dn_last = None
        for tc in range(4):
            tsl = slice(tc * 512, (tc + 1) * 512)
            hb_tok = tokens_max([pv[("hbready", k2 * 4 + tc)] for k2 in range(4)])
            for d in range(8):
                db = d % 2
                pd2 = pMB[:, 1024:1536] if db == 0 else pMA[:, 1536:2048]
                def mdn(t, d=d, tc=tc, pd2=pd2):
                    last = None
                    for k2 in range(4):
                        last = t.matmul(pd2,
                                        wbuf[:, 4096 + k2 * 1024 + d * 128:
                                             4096 + k2 * 1024 + (d + 1) * 128],
                                        hb[:, (k2 * 4 + tc) * 512:(k2 * 4 + tc + 1) * 512],
                                        start=(k2 == 0), stop=(k2 == 3))
                    return last
                t_dm = p.emit("tensor", mdn,
                              waits=[hb_tok, pv["gbsb_war"],
                                     pv.get(("pdn", db))] + lddn,
                              inc=p.sem("s_pe"))
                dn_last = t_dm
                def evd(v, d=d, tsl=tsl, pd2=pd2):
                    return v.tensor_tensor(out=x[d][:, tsl], in0=x[d][:, tsl],
                                           in1=pd2, op=ALU.add)
                t_ev = p.emit("vector", evd,
                              waits=[t_dm, sh_tok, pv.get(("xev", d, tc))],
                              inc=p.sem("s_dve"))
                pv[("pdn", db)] = t_ev
                pv[("xev", d, tc)] = t_ev
                x_toks.append(t_ev)
        pv["wdn_war"] = dn_last
        pv["pA_war"] = tokens_max([pv.get(("gbp", 0)), pv.get(("gbp", 1)),
                                   dn_last])
        pv[("pgu", 1)] = tokens_max([pv.get(("pgu", 1)), pv.get(("pdn", 1))])
        pv[("pgu", 0)] = tokens_max([pv.get(("pgu", 0)), pv.get(("pdn", 0))])
        for s_ in range(16):
            pv[("hbs", s_)] = dn_last
    ctx._moe_wbuf_war = dn_last
    ctx._moe_exp_war = dn_last
    cmB.__exit__(None, None, None)
    cmA.__exit__(None, None, None)
    return tokens_max(x_toks[-32:])



def emit_store_xf(ctx, x, out_ap, tok):
    toks = []
    for j in range(ND):
        toks.append(ctx.dma("sync", out_ap[j * 128:(j + 1) * 128, :], x[j][:],
                            waits=[tok]))
    return tokens_max(toks)


def emit_load_xf(ctx, in_ap):
    x = [ctx.sbuf([128, T], F32, "x") for _ in range(ND)]
    toks = [ctx.dma("sync", x[j][:], in_ap[j * 128:(j + 1) * 128, :])
            for j in range(ND)]
    return x, tokens_max(toks)



def _xn_pool(ctx):
    return [ctx.pool(f"xn{j}", [128, T], BF16) for j in range(ND)]


def _b_mamba(ctx, ins, outs):
    consts = emit_consts(ctx)
    W, wtok = emit_mamba_prep(ctx, ins)
    W.update(conv_w=ins["conv_w"], conv_b=ins["conv_b"], dt_bias=ins["dt_bias"],
             A_log=ins["A_log"], D=ins["D"], ssm_norm_w=ins["ssm_norm_w"],
             rms_w=ins["rms_w"])
    x, xtok = emit_load_x(ctx, consts, ins["x_in"])
    xn = _xn_pool(ctx)
    ltok = emit_ln(ctx, consts, x, tokens_max([xtok, consts["ready"]]), xn)
    mtok = emit_mamba(ctx, consts, x, xn, ltok, W, wtok)
    return [emit_store_xf(ctx, x, outs["xf"], mtok)]


def _b_moeroute(ctx, ins, outs):
    consts = emit_consts(ctx)
    x, xtok = emit_load_xf(ctx, ins["xf"])
    xn = _xn_pool(ctx)
    ltok = emit_ln(ctx, consts, x, tokens_max([xtok, consts["ready"]]), xn)
    W = {"gw_f32": ins["gate_w"]}
    rtok = emit_moe_route(ctx, consts, xn, ltok, W, None, outs["counts"],
                          outs["gf"], outs["sel"])
    return [rtok]


def _b_moeexp(ctx, ins, outs):
    consts = emit_consts(ctx)
    W, wtok = emit_moe_prep(ctx, ins)
    x, xtok = emit_load_xf(ctx, ins["xf"])
    xn = _xn_pool(ctx)
    ltok = emit_ln(ctx, consts, x, tokens_max([xtok, consts["ready"]]), xn)
    mtok = emit_moe_experts(ctx, consts, x, xn, ltok, W, wtok, ins["gf"],
                            ins["sel"], ins["counts_all"], ins["pmask8"])
    return [emit_store_xf(ctx, x, outs["xf_o"], mtok)]


def _b_linattn(ctx, ins, outs):
    consts = emit_consts(ctx)
    wqkv_bf, t1 = emit_prep_bf16(ctx, ins["w_qkv"], 1024, 3072)
    wout_bf, t2 = emit_prep_bf16(ctx, ins["w_out"], 1024, 1024)
    x, xtok = emit_load_xf(ctx, ins["xf"])
    xn = _xn_pool(ctx)
    ltok = emit_ln(ctx, consts, x, tokens_max([xtok, consts["ready"]]), xn)
    atok = emit_linattn(ctx, consts, x, xn, ltok, wqkv_bf, wout_bf,
                        tokens_max([t1, t2]))
    return [emit_store_xf(ctx, x, outs["xf_o"], atok)]


def _b_sattn(ctx, ins, outs):
    consts = emit_consts(ctx)
    wqkvT, t1 = emit_prep_transposed(ctx, ins["qkv_w"], 3072, 1024, name="wqkvT",
                                     qkv_deinterleave=True)
    fcT, t2 = emit_prep_transposed(ctx, ins["fc_w"], 1024, 1024, name="fcT")
    x, xtok = emit_load_xf(ctx, ins["xf"])
    xn = _xn_pool(ctx)
    ltok = emit_ln(ctx, consts, x, tokens_max([xtok, consts["ready"]]), xn)
    atok = emit_softmax_attn(ctx, consts, x, xn, ltok, wqkvT, fcT,
                             ins["qkv_b"], ins["fc_b"], ins["head_mask"],
                             tokens_max([t1, t2]))
    return [emit_store_xf(ctx, x, outs["xf_o"], atok)]


def _b_l1ffn(last):
    def body(ctx, ins, outs):
        consts = emit_consts(ctx)
        wgT, t1 = emit_prep_transposed(ctx, ins["w_gate"], 3072, 1024, name="wgT")
        wuT, t2 = emit_prep_transposed(ctx, ins["w_up"], 3072, 1024, name="wuT")
        wdT, t3 = emit_prep_transposed(ctx, ins["w_down"], 1024, 3072,
                                       scale=1.0 / 127.0, name="wdT")
        x, xtok = emit_load_xf(ctx, ins["xf"])
        xn = _xn_pool(ctx)
        ltok = emit_ln(ctx, consts, x, tokens_max([xtok, consts["ready"]]), xn,
                       extra_scale=1.0 / 127.0)
        ftok = emit_glu_ffn(ctx, consts, x, xn, ltok, wgT, wuT, wdT, 3072,
                            tokens_max([t1, t2, t3]))
        if last:
            return [emit_store_x(ctx, consts, x, outs["y"], ftok)]
        return [emit_store_xf(ctx, x, outs["xf_o"], ftok)]
    return body


_PROGS = {}


def _get_progs():
    if _PROGS:
        return _PROGS
    XF = ((D, T), F32)
    _PROGS["mamba"] = build_nc(
        _b_mamba,
        {"x_in": ((T, D), F32), "in_proj_w": ((4256, 1024), F32),
         "conv_w": ((2176, 4), F32), "conv_b": ((2176,), F32),
         "dt_bias": ((32,), F32), "A_log": ((32,), F32), "D": ((32,), F32),
         "ssm_norm_w": ((2048,), F32), "out_proj_w": ((1024, 2048), F32),
         "rms_w": ((1024,), F32), "fc1_w": ((8192, 1024), F32),
         "fc2_w": ((1024, 4096), F32)},
        {"xf": XF}, const_names=ALL_CONST_NAMES + ("indh",))
    _PROGS["moeroute"] = build_nc(
        _b_moeroute,
        {"xf": XF, "gate_w": ((32, 1024), F32)},
        {"counts": ((1, 32), F32), "gf": ((32, T), BF16), "sel": ((32, T), BF16)},
        const_names=ALL_CONST_NAMES)
    _PROGS["moeexp"] = build_nc(
        _b_moeexp,
        {"xf": XF, "w_gate_up": ((32, 1024, 1024), F32),
         "w_down": ((32, 512, 1024), F32),
         "shared_w_gate_up": ((1024, 1024), F32),
         "shared_w_down": ((512, 1024), F32),
         "gf": ((32, T), BF16), "sel": ((32, T), BF16),
         "counts_all": ((8, 32), F32), "pmask8": ((8, 2), F32)},
        {"xf_o": XF}, const_names=ALL_CONST_NAMES)
    _PROGS["linattn"] = build_nc(
        _b_linattn,
        {"xf": XF, "w_qkv": ((1024, 3072), F32), "w_out": ((1024, 1024), F32)},
        {"xf_o": XF}, const_names=ALL_CONST_NAMES)
    _PROGS["sattn"] = build_nc(
        _b_sattn,
        {"xf": XF, "qkv_w": ((3072, 1024), F32), "qkv_b": ((3072,), F32),
         "fc_w": ((1024, 1024), F32), "fc_b": ((1024,), F32),
         "head_mask": ((16,), F32)},
        {"xf_o": XF}, const_names=ALL_CONST_NAMES)
    _PROGS["l1ffn"] = build_nc(
        _b_l1ffn(False),
        {"xf": XF, "w_gate": ((3072, 1024), I32), "w_up": ((3072, 1024), I32),
         "w_down": ((1024, 3072), I32)},
        {"xf_o": XF}, const_names=ALL_CONST_NAMES)
    _PROGS["l1ffn_last"] = build_nc(
        _b_l1ffn(True),
        {"xf": XF, "w_gate": ((3072, 1024), I32), "w_up": ((3072, 1024), I32),
         "w_down": ((1024, 3072), I32)},
        {"y": ((T, D), F32)}, const_names=ALL_CONST_NAMES)
    return _PROGS


def _run(name, per_core, out_keys):
    nc, cmap = _get_progs()[name]
    in_maps = [dict(m, **cmap) for m in per_core]
    res = run_bass_kernel_spmd(nc, in_maps, core_ids=list(range(8)))
    return {k: [np.asarray(res.results[c][k]) for c in range(8)]
            for k in out_keys}


def kernel(**inputs):
    f32 = lambda k: np.ascontiguousarray(np.asarray(inputs[k], np.float32))
    i32 = lambda k: np.ascontiguousarray(np.asarray(inputs[k], np.int32))
    X = f32("L0_attn_a_x")

    r = _run("mamba",
             [{"x_in": X[c], "in_proj_w": f32("L0_attn_a_in_proj_w"),
               "conv_w": f32("L0_attn_a_conv_w"), "conv_b": f32("L0_attn_a_conv_b"),
               "dt_bias": f32("L0_attn_a_dt_bias"), "A_log": f32("L0_attn_a_A_log"),
               "D": f32("L0_attn_a_D"), "ssm_norm_w": f32("L0_attn_a_ssm_norm_w"),
               "out_proj_w": f32("L0_attn_a_out_proj_w"),
               "rms_w": f32("L0_attn_a_rms_w"), "fc1_w": f32("L0_attn_a_fc1_w"),
               "fc2_w": f32("L0_attn_a_fc2_w")} for c in range(8)],
             ["xf"])
    xf = r["xf"]

    def run_moe(xf):
        rr = _run("moeroute",
                  [{"xf": xf[c], "gate_w": f32("L0_ffn_gate_w")} for c in range(8)],
                  ["counts", "gf", "sel"])
        counts_all = np.ascontiguousarray(
            np.stack([rr["counts"][c][0] for c in range(8)]).astype(np.float32))
        per = []
        for c in range(8):
            pm8 = np.zeros((8, 2), np.float32)
            pm8[:c, 0] = 1.0
            per.append({"xf": xf[c], "w_gate_up": f32("L0_ffn_w_gate_up"),
                        "w_down": f32("L0_ffn_w_down"),
                        "shared_w_gate_up": f32("L0_ffn_shared_w_gate_up"),
                        "shared_w_down": f32("L0_ffn_shared_w_down"),
                        "gf": rr["gf"][c], "sel": rr["sel"][c],
                        "counts_all": counts_all, "pmask8": pm8})
        return _run("moeexp", per, ["xf_o"])["xf_o"]

    xf = run_moe(xf)
    xf = _run("linattn",
              [{"xf": xf[c], "w_qkv": f32("L0_attn_b_w_qkv"),
                "w_out": f32("L0_attn_b_w_out")} for c in range(8)],
              ["xf_o"])["xf_o"]
    xf = run_moe(xf)
    xf = _run("sattn",
              [{"xf": xf[c], "qkv_w": f32("L1_attn_a_qkv_w"),
                "qkv_b": f32("L1_attn_a_qkv_b"), "fc_w": f32("L1_attn_a_fc_w"),
                "fc_b": f32("L1_attn_a_fc_b"),
                "head_mask": f32("L1_attn_a_head_mask")} for c in range(8)],
              ["xf_o"])["xf_o"]
    xf = _run("l1ffn",
              [{"xf": xf[c], "w_gate": i32("L1_ffn_w_gate"),
                "w_up": i32("L1_ffn_w_up"), "w_down": i32("L1_ffn_w_down")}
               for c in range(8)],
              ["xf_o"])["xf_o"]
    xf = _run("sattn",
              [{"xf": xf[c], "qkv_w": f32("L1_attn_b_qkv_w"),
                "qkv_b": f32("L1_attn_b_qkv_b"), "fc_w": f32("L1_attn_b_fc_w"),
                "fc_b": f32("L1_attn_b_fc_b"),
                "head_mask": f32("L1_attn_b_head_mask")} for c in range(8)],
              ["xf_o"])["xf_o"]
    y = _run("l1ffn_last",
             [{"xf": xf[c], "w_gate": i32("L1_ffn_w_gate"),
               "w_up": i32("L1_ffn_w_up"), "w_down": i32("L1_ffn_w_down")}
              for c in range(8)],
             ["y"])["y"]
    return np.stack([np.asarray(y[c], np.float32) for c in range(8)])
